# version 15
# baseline (speedup 1.0000x reference)
import math
import numpy as np
from contextlib import ExitStack

import concourse.bass as bass
import concourse.mybir as mybir
from concourse.bass_utils import run_bass_kernel_spmd
from concourse.alu_op_type import AluOpType as OP

F32 = mybir.dt.float32
BF16 = mybir.dt.bfloat16
AF = mybir.ActivationFunctionType
AX = mybir.AxisListType

NCORES = 8
C = 2048
T_TOKENS = 4096
TC1 = 512
NCT = C // 128
EPS = 1e-5
H_CM = 8192
LAT = 512
NHEAD = 16
HD = 128
I_MOE = 1024
NEXP = 7

DEBUG_TAPS = ()



class _Opn:
    __slots__ = ("fn", "deps", "ms", "sem_val", "ring", "tag")
    def __init__(self, fn, deps, ring, tag):
        self.fn = fn
        self.deps = deps
        self.ms = False
        self.sem_val = 0
        self.ring = ring
        self.tag = tag


class Builder:
    ENGS = ("sync", "gpsimd", "tensor", "vector", "scalar")
    DMA_ENGS = ("sync", "gpsimd")
    NRING = {"sync": 8, "gpsimd": 8}

    def __init__(self):
        self.ops = {e: [] for e in self.ENGS}
        self.writer = {}
        self.readers = {}
        self.barrier_pending = {}
        self.ring_rr = {e: 0 for e in self.DMA_ENGS}

    def sem_keys(self):
        keys = [e for e in self.ENGS if e not in self.DMA_ENGS]
        for e in self.DMA_ENGS:
            keys += [f"{e}_r{i}" for i in range(self.NRING[e])]
        return keys

    def _key(self, eng, op):
        if eng in self.DMA_ENGS:
            return f"{eng}_r{op.ring}"
        return eng

    def op(self, eng, fn, reads=(), writes=(), tag=""):
        def need(dep_eng):
            return dep_eng != eng or dep_eng in self.DMA_ENGS
        deps = []
        for bname in reads:
            w = self.writer.get(bname)
            if w is not None and need(w[0]):
                deps.append(w)
        for bname in writes:
            w = self.writer.get(bname)
            if w is not None and need(w[0]):
                deps.append(w)
            for r in self.readers.get(bname, ()):
                if need(r[0]):
                    deps.append(r)
        bdeps = self.barrier_pending.pop(eng, None)
        if bdeps:
            deps.extend(bdeps)
        deps = list({d for d in deps})
        idx = len(self.ops[eng])
        for (de, di) in deps:
            self.ops[de][di].ms = True
        ring = None
        if eng in self.DMA_ENGS:
            ring = self.ring_rr[eng]
            self.ring_rr[eng] = (ring + 1) % self.NRING[eng]
        self.ops[eng].append(_Opn(fn, deps, ring, tag))
        for bname in writes:
            self.writer[bname] = (eng, idx)
            self.readers[bname] = []
        for bname in reads:
            self.readers.setdefault(bname, []).append((eng, idx))
        return (eng, idx)

    def barrier(self):
        targets = []
        for e in self.ENGS:
            if e in self.DMA_ENGS:
                last_per_ring = {}
                for i, o in enumerate(self.ops[e]):
                    last_per_ring[o.ring] = i
                for i in last_per_ring.values():
                    self.ops[e][i].ms = True
                    targets.append((e, i))
            elif self.ops[e]:
                i = len(self.ops[e]) - 1
                self.ops[e][i].ms = True
                targets.append((e, i))
        for e in self.ENGS:
            self.barrier_pending[e] = [
                t for t in targets if t[0] != e or e in self.DMA_ENGS]

    def finalize(self):
        for e in self.DMA_ENGS:
            for o in self.ops[e]:
                o.ms = True
        for e in self.ENGS:
            if e in self.DMA_ENGS:
                cnt = {i: 0 for i in range(self.NRING[e])}
                for o in self.ops[e]:
                    if o.ms:
                        cnt[o.ring] += 16
                    o.sem_val = cnt[o.ring]
            else:
                c = 0
                for o in self.ops[e]:
                    if o.ms:
                        c += 1
                    o.sem_val = c

    def run(self, eng_name, eng, sems):
        is_dma = eng_name in self.DMA_ENGS
        step = 16 if is_dma else 1
        maxw = {}
        ring_prev = {}
        for o in self.ops[eng_name]:
            if is_dma:
                pv = ring_prev.get(o.ring, 0)
                mykey = f"{eng_name}_r{o.ring}"
                if pv > 0 and maxw.get(mykey, 0) < pv:
                    eng.wait_ge(sems[mykey], pv)
                    maxw[mykey] = pv
                ring_prev[o.ring] = o.sem_val
            for (de, di) in o.deps:
                t = self.ops[de][di]
                key = self._key(de, t)
                v = t.sem_val
                assert t.ms
                if maxw.get(key, 0) < v:
                    eng.wait_ge(sems[key], v)
                    maxw[key] = v
            instr = o.fn(eng)
            if o.ms:
                instr.then_inc(sems[self._key(eng_name, o)], step)
        if is_dma:
            for r, v in ring_prev.items():
                key = f"{eng_name}_r{r}"
                if maxw.get(key, 0) < v:
                    eng.wait_ge(sems[key], v)


def _nm(ap):
    return ap.tensor.name



def _col16(v):
    return np.ascontiguousarray(np.asarray(v, np.float32).reshape(16, 128).T)


def _rope_perm():
    perm = np.zeros(1024, np.int64)
    k = 0
    for h in range(16):
        for i in range(32):
            perm[k] = h * 64 + i; k += 1
    for h in range(16):
        for i in range(32):
            perm[k] = h * 64 + 32 + i; k += 1
    return perm


def _tile_w(WT, nk, mblk):
    WT = np.ascontiguousarray(WT, dtype=np.float32)
    K, M = WT.shape
    assert K == nk * 128 and M % mblk == 0
    nb = M // mblk
    return np.ascontiguousarray(
        WT.reshape(nk, 128, nb, mblk).transpose(2, 1, 0, 3).reshape(nb * 128, nk * mblk))


def _host_prep(inp, T):
    f = np.float32
    g = {}
    g["WkT"] = _tile_w(np.asarray(inp["L0_attn_Wk"], f).T, 16, 128)
    g["WvT"] = _tile_w(np.asarray(inp["L0_attn_Wv"], f).T, 16, 128)
    g["WrT"] = _tile_w(np.asarray(inp["L0_attn_Wr"], f).T, 16, 128)
    g["WoT"] = _tile_w(np.asarray(inp["L0_attn_Wo"], f).T, 16, 512)
    g["WckT"] = _tile_w(np.asarray(inp["L0_attn_Wck"], f).T, 16, 128)
    g["WcvT"] = _tile_w(np.asarray(inp["L0_attn_Wcv"], f).T, 64, 128)
    g["WcrT"] = _tile_w(np.asarray(inp["L0_attn_Wcr"], f).T, 16, 128)
    g["ln1w"] = _col16(inp["L0_attn_ln1_w"]); g["ln1b"] = _col16(inp["L0_attn_ln1_b"])
    g["ln2w"] = _col16(inp["L0_attn_ln2_w"]); g["ln2b"] = _col16(inp["L0_attn_ln2_b"])
    g["tmk"] = _col16(inp["L0_attn_tmk"]); g["tmv"] = _col16(inp["L0_attn_tmv"])
    g["tmr"] = _col16(inp["L0_attn_tmr"])
    g["cmk"] = _col16(inp["L0_attn_cmk"]); g["cmr"] = _col16(inp["L0_attn_cmr"])
    g["td"] = _col16(inp["L0_attn_time_decay"]); g["tf"] = _col16(inp["L0_attn_time_first"])
    g["kvdW"] = _tile_w(np.asarray(inp["L0_ffn_kv_d_w"], f), 16, 512)
    g["qdW"] = _tile_w(np.asarray(inp["L0_ffn_q_d_w"], f), 16, 512)
    g["kuW"] = _tile_w(np.asarray(inp["L0_ffn_k_u_w"], f), 4, 1024)
    g["quW"] = _tile_w(np.asarray(inp["L0_ffn_q_u_w"], f), 4, 1024)
    g["vuW"] = _tile_w(np.asarray(inp["L0_ffn_v_u_w"], f), 4, 2048)
    perm = _rope_perm()
    g["rkW"] = _tile_w(np.asarray(inp["L0_ffn_rope_k_w"], f)[:, perm], 16, 512)
    g["rqW"] = _tile_w(np.asarray(inp["L0_ffn_rope_q_w"], f)[:, perm], 4, 1024)
    g["owW"] = _tile_w(np.asarray(inp["L0_ffn_o_w"], f), 16, 512)
    g["ln1wB"] = _col16(inp["L0_ffn_ln1_w"]); g["ln2wB"] = _col16(inp["L0_ffn_ln2_w"])
    g["shG"] = _tile_w(np.asarray(inp["L0_ffn_sh_gate"], f), 16, 512)
    g["shU"] = _tile_w(np.asarray(inp["L0_ffn_sh_up"], f), 16, 512)
    g["shD"] = _tile_w(np.asarray(inp["L0_ffn_sh_down"], f), 8, 512)
    g["rG"] = np.stack([_tile_w(np.asarray(inp["L0_ffn_r_gate"], f)[e], 16, 512)
                        for e in range(NEXP)])
    g["rU"] = np.stack([_tile_w(np.asarray(inp["L0_ffn_r_up"], f)[e], 16, 512)
                        for e in range(NEXP)])
    g["rD"] = np.stack([_tile_w(np.asarray(inp["L0_ffn_r_down"], f)[e], 8, 512)
                        for e in range(NEXP)])
    g["routerW"] = _tile_w(np.asarray(inp["L0_ffn_router_w"], f), 16, 7)
    g["routerB7"] = np.asarray(inp["L0_ffn_router_b"], f).reshape(1, NEXP).copy()
    g["ident"] = np.eye(128, dtype=f)
    g["ones_r"] = np.ones((1, 128), f)
    g["ones_cb"] = np.ones((128, 1), f)
    inv_freq = (10000.0 ** (-np.arange(0, 32, dtype=np.float64) / 32.0))
    t = np.arange(T, dtype=np.float64)
    ang = np.outer(t, inv_freq)
    g["cosrep"] = np.ascontiguousarray(np.cos(ang).T[np.arange(128) % 32].astype(f))
    g["sinrep"] = np.ascontiguousarray(np.sin(ang).T[np.arange(128) % 32].astype(f))
    mk = np.zeros((128, 4 * 512), f)
    for o in range(4):
        p = np.arange(128)[:, None]
        q = np.arange(512)[None, :]
        mk[:, o * 512:(o + 1) * 512] = (p + o * 128 <= q).astype(f)
    g["masks"] = mk
    return g



class K:

    def __init__(self, nc, b):
        self.nc = nc
        self.b = b

    def mm(self, out, lhsT, rhs, start=True, stop=True, tag=""):
        self.b.op("tensor",
                  lambda e, o=out, l=lhsT, r=rhs, s=start, p=stop:
                  e.matmul(o, l, r, start=s, stop=p),
                  reads=[_nm(lhsT), _nm(rhs)], writes=[_nm(out)], tag=tag)

    def trans(self, out, in_, ident, tag=""):
        self.b.op("tensor",
                  lambda e, o=out, i=in_, d=ident: e.transpose(o, i, d),
                  reads=[_nm(in_), _nm(ident)], writes=[_nm(out)], tag=tag)

    def act(self, out, in_, func, bias=None, scale=None, accum_out=None, tag=""):
        kw = {}
        if bias is not None:
            kw["bias"] = bias
        if scale is not None:
            kw["scale"] = scale
        if accum_out is not None:
            kw["accum_out"] = accum_out
        reads = [_nm(in_)]
        for x in (bias, scale):
            if isinstance(x, bass.AP):
                reads.append(_nm(x))
        writes = [_nm(out)]
        if accum_out is not None:
            writes.append(_nm(accum_out))
        self.b.op("scalar",
                  lambda e, o=out, i=in_, f=func, k=dict(kw): e.activation(o, i, f, **k),
                  reads=reads, writes=writes, tag=tag)

    def scopy(self, out, in_, tag=""):
        self.b.op("scalar", lambda e, o=out, i=in_: e.copy(o, i),
                  reads=[_nm(in_)], writes=[_nm(out)], tag=tag)

    def smul(self, out, in_, c, tag=""):
        self.b.op("scalar", lambda e, o=out, i=in_, m=c: e.mul(o, i, m),
                  reads=[_nm(in_)], writes=[_nm(out)], tag=tag)

    def sadd(self, out, in_, c, tag=""):
        self.b.op("scalar", lambda e, o=out, i=in_, m=c: e.add(o, i, m),
                  reads=[_nm(in_)], writes=[_nm(out)], tag=tag)

    def szero(self, ap, tag=""):
        self.b.op("scalar", lambda e, a=ap: e.memzero(a),
                  reads=[], writes=[_nm(ap)], tag=tag)

    def tt(self, out, a, bb, op, tag=""):
        self.b.op("vector", lambda e, o=out, x=a, y=bb, p=op: e.tensor_tensor(o, x, y, p),
                  reads=[_nm(a), _nm(bb)], writes=[_nm(out)], tag=tag)

    def ts(self, out, in_, s1, s2, op0, op1=None, tag=""):
        reads = [_nm(in_)]
        for x in (s1, s2):
            if isinstance(x, bass.AP):
                reads.append(_nm(x))
        if op1 is None:
            self.b.op("vector",
                      lambda e, o=out, i=in_, a=s1, p0=op0: e.tensor_scalar(o, i, a, None, p0),
                      reads=reads, writes=[_nm(out)], tag=tag)
        else:
            self.b.op("vector",
                      lambda e, o=out, i=in_, a=s1, c=s2, p0=op0, p1=op1:
                      e.tensor_scalar(o, i, a, c, p0, p1),
                      reads=reads, writes=[_nm(out)], tag=tag)

    def stt(self, out, in0, s, in1, op0, op1, tag=""):
        reads = [_nm(in0), _nm(in1)]
        if isinstance(s, bass.AP):
            reads.append(_nm(s))
        self.b.op("vector",
                  lambda e, o=out, a=in0, c=s, bb=in1, p0=op0, p1=op1:
                  e.scalar_tensor_tensor(o, a, c, bb, p0, p1),
                  reads=reads, writes=[_nm(out)], tag=tag)

    def scan(self, out, d0, d1, init, tag=""):
        reads = [_nm(d0), _nm(d1)]
        if isinstance(init, bass.AP):
            reads.append(_nm(init))
        self.b.op("vector",
                  lambda e, o=out, a=d0, bb=d1, i=init:
                  e.tensor_tensor_scan(o, a, bb, i, OP.mult, OP.add),
                  reads=reads, writes=[_nm(out)], tag=tag)

    def red(self, out, in_, op, tag=""):
        fn = (lambda e, o=out, i=in_: e.reduce_sum(o, i, axis=AX.X)) if op == "sum" else \
             (lambda e, o=out, i=in_: e.reduce_max(o, i, axis=AX.X))
        self.b.op("vector", fn, reads=[_nm(in_)], writes=[_nm(out)], tag=tag)

    def recip(self, out, in_, tag=""):
        self.b.op("vector", lambda e, o=out, i=in_: e.reciprocal(o, i),
                  reads=[_nm(in_)], writes=[_nm(out)], tag=tag)

    def vcopy(self, out, in_, tag=""):
        self.b.op("vector", lambda e, o=out, i=in_: e.tensor_copy(o, i),
                  reads=[_nm(in_)], writes=[_nm(out)], tag=tag)

    def dma(self, q, out, in_, tag=""):
        self.b.op(q, lambda e, o=out, i=in_: e.dma_start(out=o, in_=i),
                  reads=[_nm(in_)], writes=[_nm(out)], tag=tag)

    def ln_stats(self, x_tile, st, nelem, rms=False, tag=""):
        self.act(st["junk"], x_tile, AF.Square, accum_out=st["sq"], tag=tag)
        if rms:
            self.ts(st["vare"], st["sq"], 1.0 / nelem, None, OP.mult, tag=tag)
        else:
            self.red(st["su"], x_tile, "sum", tag=tag)
            self.scopy(st["suc"], st["su"], tag=tag)
            self.ts(st["muv"], st["suc"], 1.0 / nelem, None, OP.mult, tag=tag)
            self.scopy(st["mu"], st["muv"], tag=tag)
            self.act(st["m2"], st["muv"], AF.Square, tag=tag)
            self.stt(st["vare"], st["sq"], 1.0 / nelem, st["m2"],
                     OP.mult, OP.subtract, tag=tag)
        self.act(st["lv"], st["vare"], AF.Ln, bias=st["epsc"], tag=tag)
        self.act(st["inv"], st["lv"], AF.Exp, scale=-0.5, tag=tag)


def build_program(T, taps=()):
    nc = bass.Bass("TRN2", target_bir_lowering=False, debug=False)
    b = Builder()
    k = K(nc, b)
    NCH1 = T // TC1

    dr = {}
    def dram(name, shape, dt, kind):
        h = nc.dram_tensor(name, list(shape), dt, kind=kind)
        dr[name] = h.ap()
        return dr[name]

    x_in = dram("x", (T, C), F32, "ExternalInput")
    for nm_, sh in [("WkT", (2048, 2048)), ("WvT", (2048, 2048)),
                    ("WrT", (2048, 2048)), ("WoT", (512, 8192)),
                    ("WckT", (8192, 2048)), ("WcvT", (2048, 8192)),
                    ("WcrT", (2048, 2048)),
                    ("kvdW", (128, 8192)), ("qdW", (128, 8192)),
                    ("kuW", (128, 4096)), ("quW", (128, 4096)),
                    ("vuW", (128, 8192)), ("rkW", (256, 8192)),
                    ("rqW", (128, 4096)), ("owW", (512, 8192)),
                    ("shG", (256, 8192)), ("shU", (256, 8192)), ("shD", (512, 4096)),
                    ("rG", (NEXP, 256, 8192)), ("rU", (NEXP, 256, 8192)),
                    ("rD", (NEXP, 512, 4096))]:
        dram(nm_, sh, F32, "ExternalInput")
    for nm_ in ["ln1w", "ln1b", "ln2w", "ln2b", "tmk", "tmv", "tmr", "cmk", "cmr",
                "td", "tf", "ln1wB", "ln2wB"]:
        dram(nm_, (128, 16), F32, "ExternalInput")
    dram("routerW", (128, NCT * NEXP), F32, "ExternalInput")
    dram("routerB7", (1, NEXP), F32, "ExternalInput")
    dram("ident", (128, 128), F32, "ExternalInput")
    dram("ones_r", (1, 128), F32, "ExternalInput")
    dram("ones_cb", (128, 1), F32, "ExternalInput")
    dram("cosrep", (128, T), F32, "ExternalInput")
    dram("sinrep", (128, T), F32, "ExternalInput")
    dram("masks", (128, 4 * 512), F32, "ExternalInput")

    dram("sp_ck", (C, T), BF16, "Internal")
    dram("sp_cr", (C, T), BF16, "Internal")
    dram("sp_xsum", (T, C), F32, "Internal")
    dram("sp_rkv", (T, C), F32, "Internal")
    dram("sp_x2", (T, C), F32, "Internal")
    dram("sp_k", (C, T), BF16, "Internal")
    dram("sp_q", (C, T), BF16, "Internal")
    dram("sp_v", (T, C), BF16, "Internal")
    dram("sp_ot", (C, T), BF16, "Internal")

    WCACHE = ["WkT", "WvT", "WrT", "WoT", "WckT", "WcvT", "WcrT", "kvdW",
              "qdW", "kuW", "quW", "vuW", "rkW", "rqW", "owW", "shG", "shU",
              "shD", "rG", "rU", "rD"]
    for nm_ in WCACHE:
        dram("wc_" + nm_, tuple(dr[nm_].tensor.shape), BF16, "Internal")

    out_ap = dram("out", (T, C), F32, "ExternalOutput")

    tap_aps = {}
    for tname in taps:
        sh = (128, 64) if tname == "dbg" else (T, C)
        tap_aps[tname] = dram("tap_" + tname, sh, F32, "ExternalOutput")

    with ExitStack() as root:
        def sb(name, shape, dt=F32, ctx=None):
            return (ctx or root).enter_context(nc.sbuf_tensor(name, list(shape), dt))

        def ps(name, shape, dt=F32, ctx=None):
            return (ctx or root).enter_context(nc.psum_tensor(name, list(shape), dt))

        IDENT = sb("IDENT", (128, 128))
        ONESR = sb("ONESR", (1, 128))
        ONESCB = sb("ONESCB", (128, 1), BF16)
        PC = {}
        for nm_ in ["ln1w", "ln1b", "ln2w", "ln2b", "tmk", "tmv", "tmr", "cmk",
                    "cmr", "td", "tf", "ln1wB", "ln2wB"]:
            PC[nm_] = sb("c_" + nm_, (128, 16))
        DCOL = sb("DCOL", (128, 16))
        EUCOL = sb("EUCOL", (128, 16))
        ACAR = sb("ACAR", (128, 16))
        BCAR = sb("BCAR", (128, 16))
        CXN1 = sb("CXN1", (128, 16))
        CXN2 = sb("CXN2", (128, 16))
        ONES512 = sb("ONES512", (128, TC1))
        STAT = {}
        for nm_ in ["su", "suc", "mu", "muv", "m2", "m2ne", "vare", "lv", "inv",
                    "sq", "kk"]:
            STAT[nm_] = sb("st_" + nm_, (128, 1))[:, :]
        JUNK = None
        CONSTS = sb("CONSTS", (128, 8))
        _cmap = {}

        def cconst(val):
            val = float(val)
            if val not in _cmap:
                idx = len(_cmap)
                _cmap[val] = idx
                ap = CONSTS[:, idx:idx + 1]
                b.op("gpsimd", lambda e, a=ap, v=val: e.memset(a, v),
                     reads=[], writes=[_nm(ap)])
            return CONSTS[:, _cmap[val]:_cmap[val] + 1]

        STAT["epsc"] = cconst(EPS)
        EPS2C = cconst(EPS * EPS)

        BK = [ps(f"bank{i}", (128, 512)) for i in range(8)]

        for nm_ in PC:
            k.dma("sync", PC[nm_][:, :], dr[nm_][:, :])
        k.dma("sync", IDENT[:, :], dr["ident"][:, :])
        k.dma("sync", ONESR[:, :], dr["ones_r"][:, :])
        k.dma("gpsimd", ONESCB[:, :], dr["ones_cb"][:, :])
        k.act(DCOL[:, :], PC["td"][:, :], AF.Exp)
        k.act(DCOL[:, :], DCOL[:, :], AF.Exp, scale=-1.0)
        k.act(EUCOL[:, :], PC["tf"][:, :], AF.Exp)
        k.szero(ACAR[:, :]); k.szero(BCAR[:, :])
        k.szero(CXN1[:, :]); k.szero(CXN2[:, :])
        b.op("gpsimd", lambda e, a=ONES512[:, :]: e.memset(a, 1.0),
             reads=[], writes=[_nm(ONES512[:, :])])
        def wfirst(buf_slice, wc_slice, f32_slice):
            k.dma("gpsimd", buf_slice, f32_slice)
            k.dma("sync", wc_slice, buf_slice)

        with ExitStack() as p1:
            X0TT = sb("X0TT", (128, C), ctx=p1)
            Y0TT = sb("Y0TT", (128, C), ctx=p1)
            MUS = sb("MUS", (128, 4), ctx=p1)
            INVS = sb("INVS", (128, 4), ctx=p1)
            XC = sb("XC", (128, 4 * C), ctx=p1)
            XNT = sb("XNT", (128, NCT * (TC1 + 1)), ctx=p1)
            XK = sb("XK", (128, NCT * TC1), BF16, ctx=p1)
            XV = sb("XV", (128, NCT * TC1), BF16, ctx=p1)
            XR = sb("XR", (128, NCT * TC1), BF16, ctx=p1)
            ZT = sb("ZT", (128, NCT * TC1), BF16, ctx=p1)
            EK = sb("EK", (128, TC1), ctx=p1)
            EKV = sb("EKV", (128, TC1), ctx=p1)
            NUM = sb("NUM", (128, TC1), ctx=p1)
            DEN = sb("DEN", (128, TC1), ctx=p1)
            SRB = sb("SRB", (128, TC1), ctx=p1)
            DJ = sb("DJ", (128, TC1), ctx=p1)
            ASC = sb("ASC", (128, TC1 + 1), ctx=p1)
            BSC = sb("BSC", (128, TC1 + 1), ctx=p1)
            WKVA = sb("WKVA", (128, 3 * C), BF16, ctx=p1)
            WKVB = sb("WKVB", (128, 3 * C), BF16, ctx=p1)
            WKV = [WKVA, WKVB]
            WOA = sb("WOA", (128, 8 * 512), BF16, ctx=p1)
            WOB = sb("WOB", (128, 8 * 512), BF16, ctx=p1)
            STAT["junk"] = Y0TT[:, :]

            def p1_chunk(ch):
                t0 = ch * TC1
                for tt in range(4):
                    xs = x_in[t0 + tt * 128: t0 + (tt + 1) * 128, :]
                    k.dma("sync", X0TT[:, :], xs)
                    k.ln_stats(X0TT[:, :], STAT, C)
                    k.scopy(MUS[:, tt:tt + 1], STAT["mu"])
                    k.scopy(INVS[:, tt:tt + 1], STAT["inv"])
                    k.act(STAT["lv"], STAT["vare"], AF.Ln, bias=EPS2C,
                          scale=(1.0 + EPS))
                    k.act(STAT["kk"], STAT["lv"], AF.Exp, scale=-0.5)
                    k.ts(XC[:, tt * C:(tt + 1) * C], X0TT[:, :], STAT["mu"],
                         STAT["kk"], OP.subtract, OP.mult)
                for j in range(NCT):
                    k.scopy(XNT[:, j * (TC1 + 1): j * (TC1 + 1) + 1], CXN1[:, j:j + 1])
                pb = 0
                for tt in range(4):
                    for j in range(NCT):
                        pt = BK[6 + (pb % 2)][:, 0:128]
                        pb += 1
                        k.trans(pt, XC[:, tt * C + j * 128: tt * C + (j + 1) * 128],
                                IDENT[:, :])
                        dst = XNT[:, j * (TC1 + 1) + 1 + tt * 128:
                                  j * (TC1 + 1) + 1 + (tt + 1) * 128]
                        k.act(dst, pt, AF.Identity, scale=PC["ln1w"][:, j:j + 1],
                              bias=PC["ln1b"][:, j:j + 1])
                k.scopy(CXN1[:, :],
                        XNT[:, TC1::(TC1 + 1)])
                for j in range(NCT):
                    xn_j = XNT[:, j * (TC1 + 1) + 1: j * (TC1 + 1) + 1 + TC1]
                    xx_j = XNT[:, j * (TC1 + 1): j * (TC1 + 1) + TC1]
                    k.tt(NUM[:, :], xn_j, xx_j, OP.subtract)
                    k.stt(XK[:, j * TC1:(j + 1) * TC1], NUM[:, :],
                          PC["tmk"][:, j:j + 1], xx_j, OP.mult, OP.add)
                    k.stt(XV[:, j * TC1:(j + 1) * TC1], NUM[:, :],
                          PC["tmv"][:, j:j + 1], xx_j, OP.mult, OP.add)
                    k.stt(XR[:, j * TC1:(j + 1) * TC1], NUM[:, :],
                          PC["tmr"][:, j:j + 1], xx_j, OP.mult, OP.add)
                for j in range(NCT):
                    wb = WKV[j % 2]
                    for wi, wname in enumerate(["WkT", "WvT", "WrT"]):
                        for pz in range(2):
                            c0, c1 = pz * 1024, (pz + 1) * 1024
                            bsl = wb[:, wi * C + c0: wi * C + c1]
                            wsl = dr["wc_" + wname][j * 128:(j + 1) * 128, c0:c1]
                            if ch == 0:
                                wfirst(bsl, wsl,
                                       dr[wname][j * 128:(j + 1) * 128, c0:c1])
                            else:
                                k.dma(("sync", "gpsimd")[(wi + pz) % 2], bsl, wsl)
                    pk, pv, pr = BK[(j % 2) * 3], BK[(j % 2) * 3 + 1], BK[(j % 2) * 3 + 2]
                    for kk in range(NCT):
                        k.mm(pk[:, :], wb[:, 0 * C + kk * 128: 0 * C + (kk + 1) * 128],
                             XK[:, kk * TC1:(kk + 1) * TC1], start=(kk == 0), stop=(kk == 15))
                    for kk in range(NCT):
                        k.mm(pv[:, :], wb[:, 1 * C + kk * 128: 1 * C + (kk + 1) * 128],
                             XV[:, kk * TC1:(kk + 1) * TC1], start=(kk == 0), stop=(kk == 15))
                    for kk in range(NCT):
                        k.mm(pr[:, :], wb[:, 2 * C + kk * 128: 2 * C + (kk + 1) * 128],
                             XR[:, kk * TC1:(kk + 1) * TC1], start=(kk == 0), stop=(kk == 15))
                    k.act(EK[:, :], pk[:, :], AF.Exp)
                    k.act(SRB[:, :], pr[:, :], AF.Tanh, scale=0.5)
                    k.ts(SRB[:, :], SRB[:, :], 0.5, 0.5, OP.mult, OP.add)
                    k.tt(EKV[:, :], EK[:, :], pv[:, :], OP.mult)
                    k.ts(DJ[:, :], ONES512[:, :], DCOL[:, j:j + 1], None, OP.mult)
                    k.scopy(ASC[:, 0:1], ACAR[:, j:j + 1])
                    k.scopy(BSC[:, 0:1], BCAR[:, j:j + 1])
                    k.scan(ASC[:, 1:TC1 + 1], DJ[:, :], EKV[:, :], ASC[:, 0:1])
                    k.scan(BSC[:, 1:TC1 + 1], DJ[:, :], EK[:, :], BSC[:, 0:1])
                    k.scopy(ACAR[:, j:j + 1], ASC[:, TC1:TC1 + 1])
                    k.scopy(BCAR[:, j:j + 1], BSC[:, TC1:TC1 + 1])
                    k.stt(NUM[:, :], EKV[:, :], EUCOL[:, j:j + 1], ASC[:, 0:TC1],
                          OP.mult, OP.add)
                    k.stt(DEN[:, :], EK[:, :], EUCOL[:, j:j + 1], BSC[:, 0:TC1],
                          OP.mult, OP.add)
                    k.recip(DEN[:, :], DEN[:, :])
                    k.tt(NUM[:, :], NUM[:, :], DEN[:, :], OP.mult)
                    k.tt(ZT[:, j * TC1:(j + 1) * TC1], NUM[:, :], SRB[:, :], OP.mult)
                for dch in range(4):
                    for half, wobuf in ((0, WOA), (1, WOB)):
                        for pz in range(4):
                            c0 = half * 4096 + pz * 1024
                            bsl = wobuf[:, pz * 1024:(pz + 1) * 1024]
                            wsl = dr["wc_WoT"][dch * 128:(dch + 1) * 128, c0:c0 + 1024]
                            if ch == 0:
                                wfirst(bsl, wsl,
                                       dr["WoT"][dch * 128:(dch + 1) * 128, c0:c0 + 1024])
                            else:
                                k.dma(("sync", "gpsimd")[pz % 2], bsl, wsl)
                    for tt in range(4):
                        pw = BK[6 + (tt % 2)]
                        for kk in range(NCT):
                            wo = WOA if kk < 8 else WOB
                            k.mm(pw[:, :], ZT[:, kk * TC1 + tt * 128: kk * TC1 + (tt + 1) * 128],
                                 wo[:, (kk % 8) * 512:((kk % 8) + 1) * 512],
                                 start=(kk == 0), stop=(kk == 15))
                        xq = (NUM if (dch * 4 + tt) % 2 == 0 else DEN)[:, :]
                        k.dma("sync", xq,
                              x_in[t0 + tt * 128: t0 + (tt + 1) * 128,
                                   dch * 512:(dch + 1) * 512])
                        k.ts(xq, xq, MUS[:, tt:tt + 1],
                             INVS[:, tt:tt + 1], OP.subtract, OP.mult)
                        if "y0r" in tap_aps:
                            k.dma("gpsimd",
                                  tap_aps["y0r"][t0 + tt * 128: t0 + (tt + 1) * 128,
                                                 dch * 512:(dch + 1) * 512], xq)
                        dst = XC[:, tt * C + dch * 512: tt * C + (dch + 1) * 512]
                        k.tt(dst, xq, pw[:, :], OP.add)
                if "x1a" in tap_aps:
                    for tt in range(4):
                        k.dma("sync", tap_aps["x1a"][t0 + tt * 128: t0 + (tt + 1) * 128, :],
                              XC[:, tt * C:(tt + 1) * C])
                for j in range(NCT):
                    k.scopy(XNT[:, j * (TC1 + 1): j * (TC1 + 1) + 1], CXN2[:, j:j + 1])
                for tt in range(4):
                    xt = XC[:, tt * C:(tt + 1) * C]
                    k.ln_stats(xt, STAT, C)
                    yt = Y0TT[:, :]
                    k.ts(yt, xt, STAT["mu"], STAT["inv"], OP.subtract, OP.mult)
                    for j in range(NCT):
                        pt = BK[6 + (j % 2)][:, 0:128]
                        k.trans(pt, Y0TT[:, j * 128:(j + 1) * 128],
                                IDENT[:, :])
                        dst = XNT[:, j * (TC1 + 1) + 1 + tt * 128:
                                  j * (TC1 + 1) + 1 + (tt + 1) * 128]
                        k.act(dst, pt, AF.Identity, scale=PC["ln2w"][:, j:j + 1],
                              bias=PC["ln2b"][:, j:j + 1])
                k.scopy(CXN2[:, :], XNT[:, TC1::(TC1 + 1)])
                for j in range(NCT):
                    xn_j = XNT[:, j * (TC1 + 1) + 1: j * (TC1 + 1) + 1 + TC1]
                    xx_j = XNT[:, j * (TC1 + 1): j * (TC1 + 1) + TC1]
                    k.tt(NUM[:, :], xn_j, xx_j, OP.subtract)
                    k.stt(XK[:, j * TC1:(j + 1) * TC1], NUM[:, :],
                          PC["cmk"][:, j:j + 1], xx_j, OP.mult, OP.add)
                    k.stt(XR[:, j * TC1:(j + 1) * TC1], NUM[:, :],
                          PC["cmr"][:, j:j + 1], xx_j, OP.mult, OP.add)
                for jh in range(2):
                    r0, r1 = jh * 1024, (jh + 1) * 1024
                    ckdst = dr["sp_ck"][r0:r1, t0:t0 + TC1].rearrange(
                        "(j p) t -> p j t", p=128)
                    crdst = dr["sp_cr"][r0:r1, t0:t0 + TC1].rearrange(
                        "(j p) t -> p j t", p=128)
                    k.dma("sync", ckdst,
                          XK[:, jh * 8 * TC1:(jh + 1) * 8 * TC1]
                          .rearrange("p (j t) -> p j t", j=8))
                    k.dma("sync", crdst,
                          XR[:, jh * 8 * TC1:(jh + 1) * 8 * TC1]
                          .rearrange("p (j t) -> p j t", j=8))
                for tt in range(4):
                    r0, r1 = t0 + tt * 128, t0 + (tt + 1) * 128
                    k.dma("sync", X0TT[:, :], x_in[r0:r1, :])
                    k.tt(X0TT[:, :], X0TT[:, :], XC[:, tt * C:(tt + 1) * C], OP.add)
                    k.dma("gpsimd", dr["sp_xsum"][r0:r1, :], X0TT[:, :])
                    if "xsum" in tap_aps:
                        k.dma("gpsimd", tap_aps["xsum"][r0:r1, :], X0TT[:, :])
                if "y0" in tap_aps:
                    pass

            for ch in range(NCH1):
                p1_chunk(ch)
            b.barrier()

        _emit_tail(k, b, nc, dr, T, out_ap, tap_aps,
                   sb, ps, BK, IDENT, ONESR, ONESCB, PC, STAT, cconst)

        b.finalize()
        sems = {}
        for key in b.sem_keys():
            sems[key] = root.enter_context(nc.semaphore("sem_" + key))
        blk = root.enter_context(nc.Block())

        @blk.sync
        def _(eng):
            b.run("sync", eng, sems)

        @blk.gpsimd
        def _(eng):
            b.run("gpsimd", eng, sems)

        @blk.tensor
        def _(eng):
            b.run("tensor", eng, sems)

        @blk.vector
        def _(eng):
            b.run("vector", eng, sems)

        @blk.scalar
        def _(eng):
            b.run("scalar", eng, sems)

    return nc


def _emit_tail(k, b, nc, dr, T, out_ap, tap_aps, sb, ps, BK,
               IDENT, ONESR, ONESCB, PC, STAT, cconst):
    NCH = T // TC1
    NTT = T // 128

    def wfirst(buf_slice, wc_slice, f32_slice):
        k.dma("gpsimd", buf_slice, f32_slice)
        k.dma("sync", wc_slice, buf_slice)

    with ExitStack() as p2:
        CKB = sb("CKB", (128, NCT * TC1), BF16, ctx=p2)
        CRB = sb("CRB", (128, NCT * TC1), BF16, ctx=p2)
        HB = sb("HB", (128, 64 * TC1), BF16, ctx=p2)
        XS2 = sb("XS2", (128, 4 * C), ctx=p2)
        SR2 = sb("SR2", (128, TC1), ctx=p2)
        RKV = sb("RKV", (128, TC1), ctx=p2)
        SR22 = [sb(f"SR22_{i}", (128, TC1), ctx=p2) for i in range(2)]
        RKV2 = [sb(f"RKV2_{i}", (128, TC1), ctx=p2) for i in range(2)]
        WCK = [sb(f"WCK{i}", (128, NCT * 128), BF16, ctx=p2) for i in range(2)]
        WCR = [sb(f"WCR{i}", (128, NCT * 128), BF16, ctx=p2) for i in range(2)]
        WCV = [sb(f"WCV{i}", (128, 64 * 128), BF16, ctx=p2) for i in range(2)]
        for ch in range(NCH):
            t0 = ch * TC1
            for jh in range(2):
                r0, r1 = jh * 1024, (jh + 1) * 1024
                k.dma("sync",
                      CKB[:, jh * 8 * TC1:(jh + 1) * 8 * TC1]
                      .rearrange("p (j t) -> p j t", j=8),
                      dr["sp_ck"][r0:r1, t0:t0 + TC1].rearrange(
                          "(j p) t -> p j t", p=128))
                k.dma("sync",
                      CRB[:, jh * 8 * TC1:(jh + 1) * 8 * TC1]
                      .rearrange("p (j t) -> p j t", j=8),
                      dr["sp_cr"][r0:r1, t0:t0 + TC1].rearrange(
                          "(j p) t -> p j t", p=128))
            k.dma("sync", XS2[:, :].rearrange("p (tt c) -> p tt c", tt=4),
                  dr["sp_xsum"][t0:t0 + TC1, :].rearrange("(tt p) c -> p tt c", p=128))
            for m in range(64):
                wb = WCK[m % 2]
                if ch == 0:
                    wfirst(wb[:, :], dr["wc_WckT"][m * 128:(m + 1) * 128, :],
                           dr["WckT"][m * 128:(m + 1) * 128, :])
                else:
                    k.dma("sync", wb[:, :], dr["wc_WckT"][m * 128:(m + 1) * 128, :])
                ph = BK[m % 2]
                for kk in range(NCT):
                    k.mm(ph[:, :], wb[:, kk * 128:(kk + 1) * 128],
                         CKB[:, kk * TC1:(kk + 1) * TC1],
                         start=(kk == 0), stop=(kk == 15))
                k.act(RKV[:, :], ph[:, :], AF.Relu)
                k.tt(HB[:, m * TC1:(m + 1) * TC1], RKV[:, :], RKV[:, :], OP.mult)
            def p2_tr(jj):
                rk = RKV2[jj % 2]
                for tt in range(4):
                    pt = BK[6 + ((jj + tt) % 2)][:, 0:128]
                    k.trans(pt, rk[:, tt * 128:(tt + 1) * 128], IDENT[:, :])
                    dst = XS2[:, tt * C + jj * 128: tt * C + (jj + 1) * 128]
                    k.tt(dst, dst, pt, OP.add)
            for j in range(NCT):
                wcr = WCR[j % 2]
                if ch == 0:
                    wfirst(wcr[:, :], dr["wc_WcrT"][j * 128:(j + 1) * 128, :],
                           dr["WcrT"][j * 128:(j + 1) * 128, :])
                else:
                    k.dma("sync", wcr[:, :], dr["wc_WcrT"][j * 128:(j + 1) * 128, :])
                pr2 = BK[2 + (j % 2)]
                for kk in range(NCT):
                    k.mm(pr2[:, :], wcr[:, kk * 128:(kk + 1) * 128],
                         CRB[:, kk * TC1:(kk + 1) * TC1],
                         start=(kk == 0), stop=(kk == 15))
                k.act(SR22[j % 2][:, :], pr2[:, :], AF.Sigmoid)
                wcv = WCV[j % 2]
                if ch == 0:
                    wfirst(wcv[:, :], dr["wc_WcvT"][j * 128:(j + 1) * 128, :],
                           dr["WcvT"][j * 128:(j + 1) * 128, :])
                else:
                    k.dma("sync", wcv[:, :], dr["wc_WcvT"][j * 128:(j + 1) * 128, :])
                pkv = BK[4 + (j % 2)]
                for hk in range(64):
                    k.mm(pkv[:, :], wcv[:, hk * 128:(hk + 1) * 128],
                         HB[:, hk * TC1:(hk + 1) * TC1],
                         start=(hk == 0), stop=(hk == 63))
                k.tt(RKV2[j % 2][:, :], SR22[j % 2][:, :], pkv[:, :], OP.mult)
                if j > 0:
                    p2_tr(j - 1)
            p2_tr(NCT - 1)
            k.dma("sync",
                  dr["sp_x2"][t0:t0 + TC1, :].rearrange("(tt p) c -> p tt c", p=128),
                  XS2[:, :].rearrange("p (tt c) -> p tt c", tt=4))
            if "x2" in tap_aps:
                k.dma("sync",
                      tap_aps["x2"][t0:t0 + TC1, :].rearrange("(tt p) c -> p tt c", p=128),
                      XS2[:, :].rearrange("p (tt c) -> p tt c", tt=4))
        b.barrier()

    with ExitStack() as p2b:
        X2C = sb("X2C", (128, 4 * C), ctx=p2b)
        XAT = sb("XAT", (128, NCT * TC1), BF16, ctx=p2b)
        KVDT = sb("KVDT", (128, 4 * TC1), BF16, ctx=p2b)
        QDT = sb("QDT", (128, 4 * TC1), BF16, ctx=p2b)
        CSC = [sb(f"CSC{i}", (128, TC1), ctx=p2b) for i in range(2)]
        SSC = [sb(f"SSC{i}", (128, TC1), ctx=p2b) for i in range(2)]
        KRL = {w: sb(f"KRL_{w}", (128, 4 * TC1), ctx=p2b) for w in ("k", "q")}
        KRH = {w: sb(f"KRH_{w}", (128, 4 * TC1), ctx=p2b) for w in ("k", "q")}
        OLO = {w: sb(f"OLO_{w}", (128, 4 * TC1), BF16, ctx=p2b) for w in ("k", "q")}
        OHI = {w: sb(f"OHI_{w}", (128, 4 * TC1), BF16, ctx=p2b) for w in ("k", "q")}
        TMA = [sb(f"TMA{i}", (128, TC1), ctx=p2b) for i in range(2)]
        TMB = [sb(f"TMB{i}", (128, TC1), ctx=p2b) for i in range(2)]
        KP = [sb(f"KP{i}", (128, TC1), BF16, ctx=p2b) for i in range(2)]
        VTT = [sb(f"VTT{i}", (128, C), BF16, ctx=p2b) for i in range(2)]
        NR2 = 3
        WR2 = [sb(f"W2R{i}", (128, 8192), BF16, ctx=p2b) for i in range(NR2)]
        _wr2 = [0]

        def wload2(src_ap, ncols=8192, src32=None):
            buf = WR2[_wr2[0] % NR2]
            _wr2[0] += 1
            w = ncols // 4
            for i in range(4):
                q = ("sync", "gpsimd")[i % 2]
                if src32 is not None:
                    wfirst(buf[:, i * w:(i + 1) * w], src_ap[:, i * w:(i + 1) * w],
                           src32[:, i * w:(i + 1) * w])
                else:
                    k.dma(q, buf[:, i * w:(i + 1) * w], src_ap[:, i * w:(i + 1) * w])
            return buf

        JK2 = sb("JK2", (128, C), ctx=p2b)
        ST2 = []
        for i in range(2):
            d = {}
            for nm_ in ["su", "suc", "mu", "muv", "m2", "vare", "lv", "inv",
                        "sq", "kk"]:
                d[nm_] = sb(f"st2_{i}_{nm_}", (128, 1), ctx=p2b)[:, :]
            d["epsc"] = STAT["epsc"]
            d["junk"] = JK2[:, :]
            ST2.append(d)
        eps2c = cconst(EPS * EPS)
        for ch in range(NCH):
            t0 = ch * TC1
            csc, ssc = CSC[ch % 2], SSC[ch % 2]
            k.dma("sync", X2C[:, :].rearrange("p (tt c) -> p tt c", tt=4),
                  dr["sp_x2"][t0:t0 + TC1, :].rearrange("(tt p) c -> p tt c", p=128))
            k.dma("sync", csc[:, :], dr["cosrep"][:, t0:t0 + TC1])
            k.dma("sync", ssc[:, :], dr["sinrep"][:, t0:t0 + TC1])
            for tt in range(4):
                st = ST2[tt % 2]
                xt = X2C[:, tt * C:(tt + 1) * C]
                k.ln_stats(xt, st, C)
                k.act(st["lv"], st["vare"], AF.Ln, bias=eps2c, scale=(1.0 + EPS))
                k.act(st["kk"], st["lv"], AF.Exp, scale=-0.5)
                k.ts(xt, xt, st["mu"], st["kk"], OP.subtract, OP.mult)
            for tt in range(4):
                for j in range(NCT):
                    pt = BK[6 + (j % 2)][:, 0:128]
                    k.trans(pt, X2C[:, tt * C + j * 128: tt * C + (j + 1) * 128],
                            IDENT[:, :])
                    dst = XAT[:, j * TC1 + tt * 128: j * TC1 + (tt + 1) * 128]
                    k.act(dst, pt, AF.Identity, scale=PC["ln1wB"][:, j:j + 1])
            for wname, dst in [("kvdW", KVDT), ("qdW", QDT)]:
                wb = wload2(dr["wc_" + wname][0:128, :],
                            src32=dr[wname][0:128, :] if ch == 0 else None)
                for m4 in range(4):
                    pp = BK[m4 % 2]
                    for kk in range(NCT):
                        k.mm(pp[:, :], wb[:, kk * 512 + m4 * 128: kk * 512 + (m4 + 1) * 128],
                             XAT[:, kk * TC1:(kk + 1) * TC1],
                             start=(kk == 0), stop=(kk == 15))
                    k.scopy(dst[:, m4 * TC1:(m4 + 1) * TC1], pp[:, :])
            for hh in range(2):
                wb = wload2(dr["wc_rkW"][hh * 128:(hh + 1) * 128, :],
                            src32=dr["rkW"][hh * 128:(hh + 1) * 128, :]
                            if ch == 0 else None)
                for m4 in range(4):
                    pp = BK[2 + (m4 % 2)]
                    for kk in range(NCT):
                        k.mm(pp[:, :],
                             wb[:, kk * 512 + m4 * 128: kk * 512 + (m4 + 1) * 128],
                             XAT[:, kk * TC1:(kk + 1) * TC1],
                             start=(kk == 0), stop=(kk == 15))
                    dst = (KRL["k"] if hh == 0 else KRH["k"])
                    k.scopy(dst[:, m4 * TC1:(m4 + 1) * TC1], pp[:, :])
            wb = wload2(dr["wc_rqW"][0:128, :], ncols=4096,
                        src32=dr["rqW"][0:128, :] if ch == 0 else None)
            for m8 in range(8):
                pp = BK[2 + (m8 % 2)]
                for kk in range(4):
                    k.mm(pp[:, :],
                         wb[:, kk * 1024 + m8 * 128: kk * 1024 + (m8 + 1) * 128],
                         QDT[:, kk * TC1:(kk + 1) * TC1],
                         start=(kk == 0), stop=(kk == 3))
                dst = (KRL["q"] if m8 < 4 else KRH["q"])
                k.scopy(dst[:, (m8 % 4) * TC1:((m8 % 4) + 1) * TC1], pp[:, :])
            for wname, src, spname in [("kuW", KVDT, "sp_k"), ("quW", QDT, "sp_q")]:
                wb = wload2(dr["wc_" + wname][0:128, :], ncols=4096,
                            src32=dr[wname][0:128, :] if ch == 0 else None)
                for m8 in range(8):
                    pp = BK[4 + (m8 % 2)]
                    for kk in range(4):
                        k.mm(pp[:, :], wb[:, kk * 1024 + m8 * 128: kk * 1024 + (m8 + 1) * 128],
                             src[:, kk * TC1:(kk + 1) * TC1],
                             start=(kk == 0), stop=(kk == 3))
                    kp = KP[m8 % 2]
                    k.scopy(kp[:, :], pp[:, :])
                    k.dma("sync", dr[spname][(2 * m8) * 128:(2 * m8) * 128 + 64,
                                             t0:t0 + TC1], kp[0:64, :])
                    k.dma("sync", dr[spname][(2 * m8 + 1) * 128:(2 * m8 + 1) * 128 + 64,
                                             t0:t0 + TC1], kp[64:128, :])
            wb = wload2(dr["wc_vuW"][0:128, :],
                        src32=dr["vuW"][0:128, :] if ch == 0 else None)
            for tt in range(4):
                vtt = VTT[tt % 2]
                for dch in range(4):
                    pp = BK[(dch % 2)]
                    for kk in range(4):
                        k.mm(pp[:, :], KVDT[:, kk * TC1 + tt * 128: kk * TC1 + (tt + 1) * 128],
                             wb[:, kk * C + dch * 512: kk * C + (dch + 1) * 512],
                             start=(kk == 0), stop=(kk == 3))
                    k.scopy(vtt[:, dch * 512:(dch + 1) * 512], pp[:, :])
                k.dma("sync", dr["sp_v"][t0 + tt * 128: t0 + (tt + 1) * 128, :],
                      vtt[:, :])
            for which in ("k", "q"):
                krl, krh = KRL[which], KRH[which]
                olo, ohi = OLO[which], OHI[which]
                for g in range(4):
                    lo = krl[:, g * TC1:(g + 1) * TC1]
                    hi = krh[:, g * TC1:(g + 1) * TC1]
                    ta, tb = TMA[g % 2], TMB[g % 2]
                    k.tt(ta[:, :], lo, csc[:, :], OP.mult)
                    k.tt(tb[:, :], hi, ssc[:, :], OP.mult)
                    k.tt(olo[:, g * TC1:(g + 1) * TC1], ta[:, :], tb[:, :], OP.subtract)
                    k.tt(ta[:, :], hi, csc[:, :], OP.mult)
                    k.tt(tb[:, :], lo, ssc[:, :], OP.mult)
                    k.tt(ohi[:, g * TC1:(g + 1) * TC1], ta[:, :], tb[:, :], OP.add)
                spname = "sp_k" if which == "k" else "sp_q"
                for g in range(4):
                    for h2 in range(4):
                        row = (4 * g + h2) * 128 + 64
                        k.dma("sync", dr[spname][row:row + 32, t0:t0 + TC1],
                              olo[h2 * 32:(h2 + 1) * 32, g * TC1:(g + 1) * TC1])
                        k.dma("sync", dr[spname][row + 32:row + 64, t0:t0 + TC1],
                              ohi[h2 * 32:(h2 + 1) * 32, g * TC1:(g + 1) * TC1])
        b.barrier()

    with ExitStack() as p3:
        NKT = T // 128
        NQB = T // 512
        KT = [sb(f"KT{i}", (128, T), BF16, ctx=p3) for i in range(2)]
        QT = [sb(f"QT{i}", (128, T), BF16, ctx=p3) for i in range(2)]
        VT = [sb(f"VT{i}", (128, T), BF16, ctx=p3) for i in range(2)]
        OTH = [sb(f"OTH{i}", (128, T), BF16, ctx=p3) for i in range(2)]
        KSQB = [sb(f"KSQB{i}", (128, T), BF16, ctx=p3) for i in range(2)]
        MASKB = sb("MASKB", (128, 4 * 512), BF16, ctx=p3)
        PRB = [sb(f"PRB{i}", (128, 512), BF16, ctx=p3) for i in range(4)]
        KRALL = [sb(f"KRALL{i}", (1, T), ctx=p3) for i in range(2)]
        RLBC = [sb(f"RLBC{i}", (128, 512), ctx=p3) for i in range(2)]
        DEN3 = [sb(f"DEN3{i}", (1, 512), ctx=p3) for i in range(2)]
        RED3 = [sb(f"RED3{i}", (1, 512), ctx=p3) for i in range(4)]
        NEGCBA = sb("NEGCBA", (128, NHEAD), ctx=p3)
        SB3 = []
        for i in range(2):
            d = {nm_: sb(f"sb3_{i}_{nm_}", (1, 1), ctx=p3)[:, :]
                 for nm_ in ["kmx", "kmc", "qmx", "qmc", "prd", "cbv"]}
            SB3.append(d)
        k.dma("gpsimd", MASKB[:, :], dr["masks"][:, :])
        nlc = cconst(-math.log(math.sqrt(128.0)))
        for h in range(NHEAD):
            kt, qt = KT[h % 2], QT[h % 2]
            for pz in range(2):
                c0, c1 = pz * (T // 2), (pz + 1) * (T // 2)
                k.dma(("sync", "gpsimd")[pz], kt[:, c0:c1],
                      dr["sp_k"][h * 128:(h + 1) * 128, c0:c1])
                k.dma(("sync", "gpsimd")[1 - pz], qt[:, c0:c1],
                      dr["sp_q"][h * 128:(h + 1) * 128, c0:c1])
            sq = KSQB[h % 2]
            st3 = SB3[h % 2]
            kra = KRALL[h % 2]
            for which, srcb in (("k", kt), ("q", qt)):
                k.act(sq[:, :], srcb[:, :], AF.Square)
                for qb2 in range(NQB):
                    pb = BK[6 + (qb2 % 2)]
                    k.mm(pb[0:1, :], ONESCB[:, :], sq[:, qb2 * 512:(qb2 + 1) * 512])
                    k.scopy(kra[0:1, qb2 * 512:(qb2 + 1) * 512], pb[0:1, :])
                mx = st3["kmx"] if which == "k" else st3["qmx"]
                mc = st3["kmc"] if which == "k" else st3["qmc"]
                k.red(mx, kra[0:1, :], "max")
                k.scopy(mc, mx)
            k.tt(st3["prd"], st3["qmc"], st3["kmc"], OP.mult)
            k.act(st3["cbv"], st3["prd"], AF.Ln)
            k.act(st3["cbv"], st3["cbv"], AF.Exp, scale=0.5, bias=nlc[0:1, :])
            k.mm(BK[6 + (h % 2)][:, 0:1], ONESR[:, :], st3["cbv"])
            k.ts(NEGCBA[:, h:h + 1], BK[6 + (h % 2)][:, 0:1], -1.0, None, OP.mult)
        for h in range(NHEAD):
            kt, qt, vt = KT[h % 2], QT[h % 2], VT[h % 2]
            oth = OTH[h % 2]
            for pz in range(2):
                c0, c1 = pz * (T // 2), (pz + 1) * (T // 2)
                k.dma(("sync", "gpsimd")[pz], kt[:, c0:c1],
                      dr["sp_k"][h * 128:(h + 1) * 128, c0:c1])
                k.dma(("sync", "gpsimd")[1 - pz], qt[:, c0:c1],
                      dr["sp_q"][h * 128:(h + 1) * 128, c0:c1])
            for kg in range(0, NKT, 4):
                k.dma("sync",
                      vt[:, kg * 128:(kg + 4) * 128]
                      .rearrange("p (kb d) -> p kb d", kb=4),
                      dr["sp_v"][kg * 128:(kg + 4) * 128, h * 128:(h + 1) * 128]
                      .rearrange("(kb p) d -> p kb d", p=128))
            SCB = [BK[0], BK[1], BK[6], BK[7]]
            for qb in range(NQB):
                rs = 2 + (qb % 2)
                bs = 4 + (qb % 2)
                kbs = list(range(4 * qb, 4 * qb + 4)) + list(range(0, 4 * qb))
                nbl = len(kbs)

                def p3_lo(i, bs=bs, rs=rs, nbl=nbl, kbs=kbs):
                    kb = kbs[i]
                    pr = PRB[i % 4]
                    k.mm(BK[rs][0:1, :], ONESCB[:, :], pr[:, :],
                         start=(i == 0), stop=(i == nbl - 1))
                    k.mm(BK[bs][:, :], vt[:, kb * 128:(kb + 1) * 128], pr[:, :],
                         start=(i == 0), stop=(i == nbl - 1))
                for i in range(nbl):
                    kb = kbs[i]
                    ps_ = SCB[i % 4]
                    k.mm(ps_[:, :], kt[:, kb * 128:(kb + 1) * 128],
                         qt[:, qb * 512:(qb + 1) * 512])
                    pr = PRB[i % 4]
                    k.act(pr[:, :], ps_[:, :], AF.Exp,
                          scale=1.0 / math.sqrt(128.0), bias=NEGCBA[:, h:h + 1])
                    off = kb - 4 * qb
                    if off >= 0:
                        k.tt(pr[:, :], pr[:, :],
                             MASKB[:, off * 512:(off + 1) * 512], OP.mult)
                    if i >= 2:
                        p3_lo(i - 2)
                if nbl >= 2:
                    p3_lo(nbl - 2)
                p3_lo(nbl - 1)
                rlbc = RLBC[qb % 2]
                dsr = DEN3[qb % 2]
                k.scopy(dsr[:, :], BK[rs][0:1, :])
                pbc = BK[rs]
                k.mm(pbc[:, :], ONESR[:, :], dsr[0:1, :])
                k.recip(rlbc[:, :], pbc[:, :])
                k.tt(oth[:, qb * 512:(qb + 1) * 512], BK[bs][:, :],
                     rlbc[:, :], OP.mult)
                k.dma("sync", dr["sp_ot"][h * 128:(h + 1) * 128,
                                          qb * 512:(qb + 1) * 512],
                      oth[:, qb * 512:(qb + 1) * 512])
        b.barrier()

    with ExitStack() as p4:
        X2C = sb("X2C4", (128, 4 * C), ctx=p4)
        HX = sb("HX", (128, 4 * C), ctx=p4)
        XMT32 = sb("XMT32", (128, NCT * TC1), ctx=p4)
        OXM = sb("OXM", (128, NCT * TC1), BF16, ctx=p4)
        HE = sb("HE", (128, 8 * TC1), BF16, ctx=p4)
        NRW = 4
        WRNG = [sb(f"WR{i}", (128, 8192), BF16, ctx=p4) for i in range(NRW)]
        _wrr = [0]

        def wload(src_ap, ncols=8192, src32=None):
            buf = WRNG[_wrr[0] % NRW]
            _wrr[0] += 1
            w = ncols // 4
            for i in range(4):
                q = ("sync", "gpsimd")[i % 2]
                if src32 is not None:
                    wfirst(buf[:, i * w:(i + 1) * w], src_ap[:, i * w:(i + 1) * w],
                           src32[:, i * w:(i + 1) * w])
                else:
                    k.dma(q, buf[:, i * w:(i + 1) * w], src_ap[:, i * w:(i + 1) * w])
            return buf
        RWS = sb("RWS", (128, NCT * NEXP), ctx=p4)
        RB7 = sb("RB7", (1, NEXP), ctx=p4)
        B7BC = sb("B7BC", (128, NEXP), ctx=p4)
        SGR = sb("SGR", (128, NEXP), ctx=p4)
        SIGS = sb("SIGS", (128, NEXP), ctx=p4)
        EQ = sb("EQ", (128, NEXP), ctx=p4)
        MSK7 = sb("MSK7", (128, NEXP), ctx=p4)
        MSEL = sb("MSEL", (128, NEXP), ctx=p4)
        MSELC = sb("MSELC", (128, NEXP), ctx=p4)
        W7A = sb("W7A", (128, NEXP), ctx=p4)
        W7AC = sb("W7AC", (128, NEXP), ctx=p4)
        W7B = sb("W7B", (128, NEXP), ctx=p4)
        W7C = sb("W7C", (128, 4 * NEXP), ctx=p4)
        SGF = sb("SGF", (128, TC1), ctx=p4)
        SM1 = {}
        for nm_ in ["b1", "b1c", "cnt", "cntc", "b2", "b2c", "ge2", "ge2c",
                    "d12", "d12c", "adj", "adjc", "b2f", "b2fc", "den",
                    "denc", "rd", "rd8"]:
            SM1[nm_] = sb("sm_" + nm_, (128, 1), ctx=p4)[:, :]
        JUNK4 = sb("JUNK4", (128, C), ctx=p4)
        STAT["junk"] = JUNK4[:, :]
        k.dma("sync", RWS[:, :], dr["routerW"][:, :])
        k.dma("sync", RB7[:, :], dr["routerB7"][:, :])
        k.mm(BK[5][:, 0:NEXP], ONESR[:, :], RB7[:, :])
        k.scopy(B7BC[:, :], BK[5][:, 0:NEXP])
        for ch in range(NCH):
            t0 = ch * TC1
            k.dma("sync", X2C[:, :].rearrange("p (tt c) -> p tt c", tt=4),
                  dr["sp_x2"][t0:t0 + TC1, :].rearrange("(tt p) c -> p tt c", p=128))
            for jh in range(2):
                r0, r1 = jh * 1024, (jh + 1) * 1024
                k.dma("sync",
                      OXM[:, jh * 8 * TC1:(jh + 1) * 8 * TC1]
                      .rearrange("p (j t) -> p j t", j=8),
                      dr["sp_ot"][r0:r1, t0:t0 + TC1].rearrange(
                          "(j p) t -> p j t", p=128))
            for tt in range(4):
                xt = X2C[:, tt * C:(tt + 1) * C]
                k.ln_stats(xt, STAT, C)
                k.ts(HX[:, tt * C:(tt + 1) * C], xt, STAT["mu"], STAT["inv"],
                     OP.subtract, OP.mult)
            for dch in range(4):
                wow = wload(dr["wc_owW"][dch * 128:(dch + 1) * 128, :],
                            src32=dr["owW"][dch * 128:(dch + 1) * 128, :]
                            if ch == 0 else None)
                for tt in range(4):
                    pw = BK[(tt % 2)]
                    for hk in range(NCT):
                        k.mm(pw[:, :], OXM[:, hk * TC1 + tt * 128: hk * TC1 + (tt + 1) * 128],
                             wow[:, hk * 512:(hk + 1) * 512],
                             start=(hk == 0), stop=(hk == 15))
                    dst = HX[:, tt * C + dch * 512: tt * C + (dch + 1) * 512]
                    k.tt(dst, dst, pw[:, :], OP.add)
            if "h" in tap_aps:
                for tt in range(4):
                    k.dma("sync", tap_aps["h"][t0 + tt * 128: t0 + (tt + 1) * 128, :],
                          HX[:, tt * C:(tt + 1) * C])
            for tt in range(4):
                k.tt(X2C[:, tt * C:(tt + 1) * C], X2C[:, tt * C:(tt + 1) * C],
                     HX[:, tt * C:(tt + 1) * C], OP.add)
            for tt in range(4):
                ht = HX[:, tt * C:(tt + 1) * C]
                k.ln_stats(ht, STAT, C, rms=True)
                k.ts(ht, ht, STAT["inv"], None, OP.mult)
            for tt in range(4):
                for j in range(NCT):
                    pt = BK[6 + (j % 2)][:, 0:128]
                    k.trans(pt, HX[:, tt * C + j * 128: tt * C + (j + 1) * 128],
                            IDENT[:, :])
                    dst = XMT32[:, j * TC1 + tt * 128: j * TC1 + (tt + 1) * 128]
                    k.ts(dst, pt, PC["ln2wB"][:, j:j + 1], None, OP.mult)
            k.scopy(OXM[:, :], XMT32[:, :])
            for tt in range(4):
                prr = BK[5][:, 0:NEXP]
                for kk in range(NCT):
                    k.mm(prr, XMT32[:, kk * TC1 + tt * 128: kk * TC1 + (tt + 1) * 128],
                         RWS[:, kk * NEXP:(kk + 1) * NEXP],
                         start=(kk == 0), stop=(kk == 15))
                k.tt(SGR[:, :], prr, B7BC[:, :], OP.add)
                k.act(SIGS[:, :], SGR[:, :], AF.Sigmoid)
                k.red(SM1["b1"], SIGS[:, :], "max")
                k.scopy(SM1["b1c"], SM1["b1"])
                k.ts(EQ[:, :], SIGS[:, :], SM1["b1c"], None, OP.is_equal)
                k.scopy(MSK7[:, :], EQ[:, :])
                k.red(SM1["cnt"], MSK7[:, :], "sum")
                k.scopy(SM1["cntc"], SM1["cnt"])
                k.stt(MSK7[:, :], MSK7[:, :], -1e30, SIGS[:, :], OP.mult, OP.add)
                k.scopy(EQ[:, :], MSK7[:, :])
                k.red(SM1["b2"], EQ[:, :], "max")
                k.scopy(SM1["b2c"], SM1["b2"])
                k.ts(SM1["ge2"], SM1["cntc"], 1.5, None, OP.is_ge)
                k.scopy(SM1["ge2c"], SM1["ge2"])
                k.tt(SM1["d12"], SM1["b1c"], SM1["b2c"], OP.subtract)
                k.scopy(SM1["d12c"], SM1["d12"])
                k.tt(SM1["adj"], SM1["ge2c"], SM1["d12c"], OP.mult)
                k.scopy(SM1["adjc"], SM1["adj"])
                k.tt(SM1["b2f"], SM1["b2c"], SM1["adjc"], OP.add)
                k.scopy(SM1["b2fc"], SM1["b2f"])
                k.tt(SM1["den"], SM1["b1c"], SM1["b2fc"], OP.add)
                k.scopy(SM1["denc"], SM1["den"])
                k.recip(SM1["rd"], SM1["denc"])
                k.scopy(SM1["rd8"], SM1["rd"])
                k.smul(SM1["rd8"], SM1["rd8"], 0.125)
                k.ts(MSEL[:, :], SIGS[:, :], SM1["b2fc"], None, OP.is_ge)
                k.scopy(MSELC[:, :], MSEL[:, :])
                k.tt(W7A[:, :], SIGS[:, :], MSELC[:, :], OP.mult)
                k.scopy(W7AC[:, :], W7A[:, :])
                k.ts(W7B[:, :], W7AC[:, :], SM1["rd8"], None, OP.mult)
                k.scopy(W7C[:, tt * NEXP:(tt + 1) * NEXP], W7B[:, :])
            for e in range(8):
                if e < NEXP:
                    gsrc = dr["wc_rG"][e, :, :]; usrc = dr["wc_rU"][e, :, :]
                    dsrc = dr["wc_rD"][e, :, :]
                    g32 = dr["rG"][e, :, :]; u32 = dr["rU"][e, :, :]
                    d32 = dr["rD"][e, :, :]
                else:
                    gsrc = dr["wc_shG"][:, :]; usrc = dr["wc_shU"][:, :]
                    dsrc = dr["wc_shD"][:, :]
                    g32 = dr["shG"][:, :]; u32 = dr["shU"][:, :]
                    d32 = dr["shD"][:, :]
                fst = (ch == 0)
                for half in range(2):
                    wg = wload(gsrc[half * 128:(half + 1) * 128, :],
                               src32=g32[half * 128:(half + 1) * 128, :] if fst else None)
                    wu = wload(usrc[half * 128:(half + 1) * 128, :],
                               src32=u32[half * 128:(half + 1) * 128, :] if fst else None)
                    for m4 in range(4):
                        im = half * 4 + m4
                        pg = BK[2 + 4 * (im % 2)]; pu = BK[3 + 4 * (im % 2)]
                        for kk in range(NCT):
                            k.mm(pg[:, :], wg[:, kk * 512 + m4 * 128: kk * 512 + (m4 + 1) * 128],
                                 OXM[:, kk * TC1:(kk + 1) * TC1],
                                 start=(kk == 0), stop=(kk == 15))
                        k.act(SGF[:, :], pg[:, :], AF.Silu)
                        for kk in range(NCT):
                            k.mm(pu[:, :], wu[:, kk * 512 + m4 * 128: kk * 512 + (m4 + 1) * 128],
                                 OXM[:, kk * TC1:(kk + 1) * TC1],
                                 start=(kk == 0), stop=(kk == 15))
                        k.tt(HE[:, im * TC1:(im + 1) * TC1], SGF[:, :], pu[:, :],
                             OP.mult)
                for cch in range(4):
                    wd = wload(dsrc[cch * 128:(cch + 1) * 128, :], ncols=4096,
                               src32=d32[cch * 128:(cch + 1) * 128, :] if fst else None)
                    for tt in range(4):
                        pd = BK[4] if tt % 2 == 0 else BK[5]
                        for ik in range(8):
                            k.mm(pd[:, :], HE[:, ik * TC1 + tt * 128: ik * TC1 + (tt + 1) * 128],
                                 wd[:, ik * 512:(ik + 1) * 512],
                                 start=(ik == 0), stop=(ik == 7))
                        dst = X2C[:, tt * C + cch * 512: tt * C + (cch + 1) * 512]
                        if e < NEXP:
                            k.stt(dst, pd[:, :], W7C[:, tt * NEXP + e: tt * NEXP + e + 1],
                                  dst, OP.mult, OP.add)
                        else:
                            k.stt(dst, pd[:, :], 0.125, dst, OP.mult, OP.add)
            k.dma("sync",
                  out_ap[t0:t0 + TC1, :].rearrange("(tt p) c -> p tt c", p=128),
                  X2C[:, :].rearrange("p (tt c) -> p tt c", tt=4))
        b.barrier()



def kernel(**inputs):
    T = T_TOKENS
    x = np.asarray(inputs["L0_attn_x"], np.float32)
    g = _host_prep(inputs, T)
    nc = build_program(T, taps=DEBUG_TAPS)
    in_maps = []
    for bq in range(NCORES):
        m = dict(g)
        m["x"] = np.ascontiguousarray(x[bq, :T, :])
        in_maps.append(m)
    res = run_bass_kernel_spmd(nc, in_maps, core_ids=list(range(NCORES)))
    out = np.stack([res.results[bq]["out"] for bq in range(NCORES)], axis=0)
    return out


# version 16
# speedup vs baseline: 1.1736x; 1.1736x over previous
import math
import numpy as np
from contextlib import ExitStack

import concourse.bass as bass
import concourse.mybir as mybir
from concourse.bass_utils import run_bass_kernel_spmd
from concourse.alu_op_type import AluOpType as OP

F32 = mybir.dt.float32
BF16 = mybir.dt.bfloat16
AF = mybir.ActivationFunctionType
AX = mybir.AxisListType

NCORES = 8
C = 2048
T_TOKENS = 4096
TC1 = 512
NCT = C // 128
EPS = 1e-5
H_CM = 8192
LAT = 512
NHEAD = 16
HD = 128
I_MOE = 1024
NEXP = 7

DEBUG_TAPS = ()



class _Opn:
    __slots__ = ("fn", "deps", "ms", "sem_val", "ring", "tag")
    def __init__(self, fn, deps, ring, tag):
        self.fn = fn
        self.deps = deps
        self.ms = False
        self.sem_val = 0
        self.ring = ring
        self.tag = tag


class Builder:
    ENGS = ("sync", "gpsimd", "tensor", "vector", "scalar")
    DMA_ENGS = ("sync", "gpsimd")
    NRING = {"sync": 8, "gpsimd": 8}

    def __init__(self):
        self.ops = {e: [] for e in self.ENGS}
        self.writer = {}
        self.readers = {}
        self.barrier_pending = {}
        self.ring_rr = {e: 0 for e in self.DMA_ENGS}

    def sem_keys(self):
        keys = [e for e in self.ENGS if e not in self.DMA_ENGS]
        for e in self.DMA_ENGS:
            keys += [f"{e}_r{i}" for i in range(self.NRING[e])]
        return keys

    def _key(self, eng, op):
        if eng in self.DMA_ENGS:
            return f"{eng}_r{op.ring}"
        return eng

    def op(self, eng, fn, reads=(), writes=(), tag=""):
        def need(dep_eng):
            return dep_eng != eng or dep_eng in self.DMA_ENGS
        deps = []
        for bname in reads:
            w = self.writer.get(bname)
            if w is not None and need(w[0]):
                deps.append(w)
        for bname in writes:
            w = self.writer.get(bname)
            if w is not None and need(w[0]):
                deps.append(w)
            for r in self.readers.get(bname, ()):
                if need(r[0]):
                    deps.append(r)
        bdeps = self.barrier_pending.pop(eng, None)
        if bdeps:
            deps.extend(bdeps)
        deps = list({d for d in deps})
        idx = len(self.ops[eng])
        for (de, di) in deps:
            self.ops[de][di].ms = True
        ring = None
        if eng in self.DMA_ENGS:
            ring = self.ring_rr[eng]
            self.ring_rr[eng] = (ring + 1) % self.NRING[eng]
        self.ops[eng].append(_Opn(fn, deps, ring, tag))
        for bname in writes:
            self.writer[bname] = (eng, idx)
            self.readers[bname] = []
        for bname in reads:
            self.readers.setdefault(bname, []).append((eng, idx))
        return (eng, idx)

    def barrier(self):
        targets = []
        for e in self.ENGS:
            if e in self.DMA_ENGS:
                last_per_ring = {}
                for i, o in enumerate(self.ops[e]):
                    last_per_ring[o.ring] = i
                for i in last_per_ring.values():
                    self.ops[e][i].ms = True
                    targets.append((e, i))
            elif self.ops[e]:
                i = len(self.ops[e]) - 1
                self.ops[e][i].ms = True
                targets.append((e, i))
        for e in self.ENGS:
            self.barrier_pending[e] = [
                t for t in targets if t[0] != e or e in self.DMA_ENGS]

    def finalize(self):
        for e in self.DMA_ENGS:
            for o in self.ops[e]:
                o.ms = True
        for e in self.ENGS:
            if e in self.DMA_ENGS:
                cnt = {i: 0 for i in range(self.NRING[e])}
                for o in self.ops[e]:
                    if o.ms:
                        cnt[o.ring] += 16
                    o.sem_val = cnt[o.ring]
            else:
                c = 0
                for o in self.ops[e]:
                    if o.ms:
                        c += 1
                    o.sem_val = c

    def run(self, eng_name, eng, sems):
        is_dma = eng_name in self.DMA_ENGS
        step = 16 if is_dma else 1
        maxw = {}
        ring_prev = {}
        for o in self.ops[eng_name]:
            if is_dma:
                pv = ring_prev.get(o.ring, 0)
                mykey = f"{eng_name}_r{o.ring}"
                if pv > 0 and maxw.get(mykey, 0) < pv:
                    eng.wait_ge(sems[mykey], pv)
                    maxw[mykey] = pv
                ring_prev[o.ring] = o.sem_val
            for (de, di) in o.deps:
                t = self.ops[de][di]
                key = self._key(de, t)
                v = t.sem_val
                assert t.ms
                if maxw.get(key, 0) < v:
                    eng.wait_ge(sems[key], v)
                    maxw[key] = v
            instr = o.fn(eng)
            if o.ms:
                instr.then_inc(sems[self._key(eng_name, o)], step)
        if is_dma:
            for r, v in ring_prev.items():
                key = f"{eng_name}_r{r}"
                if maxw.get(key, 0) < v:
                    eng.wait_ge(sems[key], v)


def _nm(ap):
    return ap.tensor.name



def _col16(v):
    return np.ascontiguousarray(np.asarray(v, np.float32).reshape(16, 128).T)


def _rope_perm():
    perm = np.zeros(1024, np.int64)
    k = 0
    for h in range(16):
        for i in range(32):
            perm[k] = h * 64 + i; k += 1
    for h in range(16):
        for i in range(32):
            perm[k] = h * 64 + 32 + i; k += 1
    return perm


def _tile_w(WT, nk, mblk):
    WT = np.ascontiguousarray(WT, dtype=np.float32)
    K, M = WT.shape
    assert K == nk * 128 and M % mblk == 0
    nb = M // mblk
    return np.ascontiguousarray(
        WT.reshape(nk, 128, nb, mblk).transpose(2, 1, 0, 3).reshape(nb * 128, nk * mblk))


def _host_prep(inp, T):
    f = np.float32
    g = {}
    g["WkT"] = _tile_w(np.asarray(inp["L0_attn_Wk"], f).T, 16, 128)
    g["WvT"] = _tile_w(np.asarray(inp["L0_attn_Wv"], f).T, 16, 128)
    g["WrT"] = _tile_w(np.asarray(inp["L0_attn_Wr"], f).T, 16, 128)
    g["WoT"] = _tile_w(np.asarray(inp["L0_attn_Wo"], f).T, 16, 512)
    g["WckT"] = _tile_w(np.asarray(inp["L0_attn_Wck"], f).T, 16, 128)
    g["WcvT"] = _tile_w(np.asarray(inp["L0_attn_Wcv"], f).T, 64, 128)
    g["WcrT"] = _tile_w(np.asarray(inp["L0_attn_Wcr"], f).T, 16, 128)
    g["ln1w"] = _col16(inp["L0_attn_ln1_w"]); g["ln1b"] = _col16(inp["L0_attn_ln1_b"])
    g["ln2w"] = _col16(inp["L0_attn_ln2_w"]); g["ln2b"] = _col16(inp["L0_attn_ln2_b"])
    g["tmk"] = _col16(inp["L0_attn_tmk"]); g["tmv"] = _col16(inp["L0_attn_tmv"])
    g["tmr"] = _col16(inp["L0_attn_tmr"])
    g["cmk"] = _col16(inp["L0_attn_cmk"]); g["cmr"] = _col16(inp["L0_attn_cmr"])
    g["td"] = _col16(inp["L0_attn_time_decay"]); g["tf"] = _col16(inp["L0_attn_time_first"])
    g["kvdW"] = _tile_w(np.asarray(inp["L0_ffn_kv_d_w"], f), 16, 512)
    g["qdW"] = _tile_w(np.asarray(inp["L0_ffn_q_d_w"], f), 16, 512)
    g["kuW"] = _tile_w(np.asarray(inp["L0_ffn_k_u_w"], f), 4, 1024)
    g["quW"] = _tile_w(np.asarray(inp["L0_ffn_q_u_w"], f), 4, 1024)
    g["vuW"] = _tile_w(np.asarray(inp["L0_ffn_v_u_w"], f), 4, 2048)
    perm = _rope_perm()
    g["rkW"] = _tile_w(np.asarray(inp["L0_ffn_rope_k_w"], f)[:, perm], 16, 512)
    g["rqW"] = _tile_w(np.asarray(inp["L0_ffn_rope_q_w"], f)[:, perm], 4, 1024)
    g["owW"] = _tile_w(np.asarray(inp["L0_ffn_o_w"], f), 16, 512)
    g["ln1wB"] = _col16(inp["L0_ffn_ln1_w"]); g["ln2wB"] = _col16(inp["L0_ffn_ln2_w"])
    g["shG"] = _tile_w(np.asarray(inp["L0_ffn_sh_gate"], f), 16, 512)
    g["shU"] = _tile_w(np.asarray(inp["L0_ffn_sh_up"], f), 16, 512)
    g["shD"] = _tile_w(np.asarray(inp["L0_ffn_sh_down"], f), 8, 512)
    g["rG"] = np.stack([_tile_w(np.asarray(inp["L0_ffn_r_gate"], f)[e], 16, 512)
                        for e in range(NEXP)])
    g["rU"] = np.stack([_tile_w(np.asarray(inp["L0_ffn_r_up"], f)[e], 16, 512)
                        for e in range(NEXP)])
    g["rD"] = np.stack([_tile_w(np.asarray(inp["L0_ffn_r_down"], f)[e], 8, 512)
                        for e in range(NEXP)])
    g["routerW"] = _tile_w(np.asarray(inp["L0_ffn_router_w"], f), 16, 7)
    g["routerB7"] = np.asarray(inp["L0_ffn_router_b"], f).reshape(1, NEXP).copy()
    g["ident"] = np.eye(128, dtype=f)
    g["ones_r"] = np.ones((1, 128), f)
    g["ones_cb"] = np.ones((128, 1), f)
    inv_freq = (10000.0 ** (-np.arange(0, 32, dtype=np.float64) / 32.0))
    t = np.arange(T, dtype=np.float64)
    ang = np.outer(t, inv_freq)
    g["cosrep"] = np.ascontiguousarray(np.cos(ang).T[np.arange(128) % 32].astype(f))
    g["sinrep"] = np.ascontiguousarray(np.sin(ang).T[np.arange(128) % 32].astype(f))
    mk = np.zeros((128, 4 * 512), f)
    for o in range(4):
        p = np.arange(128)[:, None]
        q = np.arange(512)[None, :]
        mk[:, o * 512:(o + 1) * 512] = (p + o * 128 <= q).astype(f)
    g["masks"] = mk
    return g



class K:

    def __init__(self, nc, b):
        self.nc = nc
        self.b = b

    def mm(self, out, lhsT, rhs, start=True, stop=True, tag=""):
        self.b.op("tensor",
                  lambda e, o=out, l=lhsT, r=rhs, s=start, p=stop:
                  e.matmul(o, l, r, start=s, stop=p),
                  reads=[_nm(lhsT), _nm(rhs)], writes=[_nm(out)], tag=tag)

    def trans(self, out, in_, ident, tag=""):
        self.b.op("tensor",
                  lambda e, o=out, i=in_, d=ident: e.transpose(o, i, d),
                  reads=[_nm(in_), _nm(ident)], writes=[_nm(out)], tag=tag)

    def act(self, out, in_, func, bias=None, scale=None, accum_out=None, tag=""):
        kw = {}
        if bias is not None:
            kw["bias"] = bias
        if scale is not None:
            kw["scale"] = scale
        if accum_out is not None:
            kw["accum_out"] = accum_out
        reads = [_nm(in_)]
        for x in (bias, scale):
            if isinstance(x, bass.AP):
                reads.append(_nm(x))
        writes = [_nm(out)]
        if accum_out is not None:
            writes.append(_nm(accum_out))
        self.b.op("scalar",
                  lambda e, o=out, i=in_, f=func, k=dict(kw): e.activation(o, i, f, **k),
                  reads=reads, writes=writes, tag=tag)

    def scopy(self, out, in_, tag=""):
        self.b.op("scalar", lambda e, o=out, i=in_: e.copy(o, i),
                  reads=[_nm(in_)], writes=[_nm(out)], tag=tag)

    def smul(self, out, in_, c, tag=""):
        self.b.op("scalar", lambda e, o=out, i=in_, m=c: e.mul(o, i, m),
                  reads=[_nm(in_)], writes=[_nm(out)], tag=tag)

    def sadd(self, out, in_, c, tag=""):
        self.b.op("scalar", lambda e, o=out, i=in_, m=c: e.add(o, i, m),
                  reads=[_nm(in_)], writes=[_nm(out)], tag=tag)

    def szero(self, ap, tag=""):
        self.b.op("scalar", lambda e, a=ap: e.memzero(a),
                  reads=[], writes=[_nm(ap)], tag=tag)

    def tt(self, out, a, bb, op, tag=""):
        self.b.op("vector", lambda e, o=out, x=a, y=bb, p=op: e.tensor_tensor(o, x, y, p),
                  reads=[_nm(a), _nm(bb)], writes=[_nm(out)], tag=tag)

    def ts(self, out, in_, s1, s2, op0, op1=None, tag=""):
        reads = [_nm(in_)]
        for x in (s1, s2):
            if isinstance(x, bass.AP):
                reads.append(_nm(x))
        if op1 is None:
            self.b.op("vector",
                      lambda e, o=out, i=in_, a=s1, p0=op0: e.tensor_scalar(o, i, a, None, p0),
                      reads=reads, writes=[_nm(out)], tag=tag)
        else:
            self.b.op("vector",
                      lambda e, o=out, i=in_, a=s1, c=s2, p0=op0, p1=op1:
                      e.tensor_scalar(o, i, a, c, p0, p1),
                      reads=reads, writes=[_nm(out)], tag=tag)

    def stt(self, out, in0, s, in1, op0, op1, tag=""):
        reads = [_nm(in0), _nm(in1)]
        if isinstance(s, bass.AP):
            reads.append(_nm(s))
        self.b.op("vector",
                  lambda e, o=out, a=in0, c=s, bb=in1, p0=op0, p1=op1:
                  e.scalar_tensor_tensor(o, a, c, bb, p0, p1),
                  reads=reads, writes=[_nm(out)], tag=tag)

    def scan(self, out, d0, d1, init, tag=""):
        reads = [_nm(d0), _nm(d1)]
        if isinstance(init, bass.AP):
            reads.append(_nm(init))
        self.b.op("vector",
                  lambda e, o=out, a=d0, bb=d1, i=init:
                  e.tensor_tensor_scan(o, a, bb, i, OP.mult, OP.add),
                  reads=reads, writes=[_nm(out)], tag=tag)

    def red(self, out, in_, op, tag=""):
        fn = (lambda e, o=out, i=in_: e.reduce_sum(o, i, axis=AX.X)) if op == "sum" else \
             (lambda e, o=out, i=in_: e.reduce_max(o, i, axis=AX.X))
        self.b.op("vector", fn, reads=[_nm(in_)], writes=[_nm(out)], tag=tag)

    def recip(self, out, in_, tag=""):
        self.b.op("vector", lambda e, o=out, i=in_: e.reciprocal(o, i),
                  reads=[_nm(in_)], writes=[_nm(out)], tag=tag)

    def vcopy(self, out, in_, tag=""):
        self.b.op("vector", lambda e, o=out, i=in_: e.tensor_copy(o, i),
                  reads=[_nm(in_)], writes=[_nm(out)], tag=tag)

    def dma(self, q, out, in_, tag=""):
        self.b.op(q, lambda e, o=out, i=in_: e.dma_start(out=o, in_=i),
                  reads=[_nm(in_)], writes=[_nm(out)], tag=tag)

    def ln_stats(self, x_tile, st, nelem, rms=False, tag=""):
        self.act(st["junk"], x_tile, AF.Square, accum_out=st["sq"], tag=tag)
        if rms:
            self.ts(st["vare"], st["sq"], 1.0 / nelem, None, OP.mult, tag=tag)
        else:
            self.red(st["su"], x_tile, "sum", tag=tag)
            self.scopy(st["suc"], st["su"], tag=tag)
            self.ts(st["muv"], st["suc"], 1.0 / nelem, None, OP.mult, tag=tag)
            self.scopy(st["mu"], st["muv"], tag=tag)
            self.act(st["m2"], st["muv"], AF.Square, tag=tag)
            self.stt(st["vare"], st["sq"], 1.0 / nelem, st["m2"],
                     OP.mult, OP.subtract, tag=tag)
        self.act(st["lv"], st["vare"], AF.Ln, bias=st["epsc"], tag=tag)
        self.act(st["inv"], st["lv"], AF.Exp, scale=-0.5, tag=tag)


def build_program(T, taps=()):
    nc = bass.Bass("TRN2", target_bir_lowering=False, debug=False)
    b = Builder()
    k = K(nc, b)
    NCH1 = T // TC1

    dr = {}
    def dram(name, shape, dt, kind):
        h = nc.dram_tensor(name, list(shape), dt, kind=kind)
        dr[name] = h.ap()
        return dr[name]

    x_in = dram("x", (T, C), F32, "ExternalInput")
    for nm_, sh in [("WkT", (2048, 2048)), ("WvT", (2048, 2048)),
                    ("WrT", (2048, 2048)), ("WoT", (512, 8192)),
                    ("WckT", (8192, 2048)), ("WcvT", (2048, 8192)),
                    ("WcrT", (2048, 2048)),
                    ("kvdW", (128, 8192)), ("qdW", (128, 8192)),
                    ("kuW", (128, 4096)), ("quW", (128, 4096)),
                    ("vuW", (128, 8192)), ("rkW", (256, 8192)),
                    ("rqW", (128, 4096)), ("owW", (512, 8192)),
                    ("shG", (256, 8192)), ("shU", (256, 8192)), ("shD", (512, 4096)),
                    ("rG", (NEXP, 256, 8192)), ("rU", (NEXP, 256, 8192)),
                    ("rD", (NEXP, 512, 4096))]:
        dram(nm_, sh, F32, "ExternalInput")
    for nm_ in ["ln1w", "ln1b", "ln2w", "ln2b", "tmk", "tmv", "tmr", "cmk", "cmr",
                "td", "tf", "ln1wB", "ln2wB"]:
        dram(nm_, (128, 16), F32, "ExternalInput")
    dram("routerW", (128, NCT * NEXP), F32, "ExternalInput")
    dram("routerB7", (1, NEXP), F32, "ExternalInput")
    dram("ident", (128, 128), F32, "ExternalInput")
    dram("ones_r", (1, 128), F32, "ExternalInput")
    dram("ones_cb", (128, 1), F32, "ExternalInput")
    dram("cosrep", (128, T), F32, "ExternalInput")
    dram("sinrep", (128, T), F32, "ExternalInput")
    dram("masks", (128, 4 * 512), F32, "ExternalInput")

    dram("sp_ck", (C, T), BF16, "Internal")
    dram("sp_cr", (C, T), BF16, "Internal")
    dram("sp_xsum", (T, C), F32, "Internal")
    dram("sp_rkv", (T, C), F32, "Internal")
    dram("sp_x2", (T, C), F32, "Internal")
    dram("sp_k", (C, T), BF16, "Internal")
    dram("sp_q", (C, T), BF16, "Internal")
    dram("sp_v", (T, C), BF16, "Internal")
    dram("sp_ot", (C, T), BF16, "Internal")

    WCACHE = ["WkT", "WvT", "WrT", "WoT", "WckT", "WcvT", "WcrT", "kvdW",
              "qdW", "kuW", "quW", "vuW", "rkW", "rqW", "owW", "shG", "shU",
              "shD", "rG", "rU", "rD"]
    for nm_ in WCACHE:
        dram("wc_" + nm_, tuple(dr[nm_].tensor.shape), BF16, "Internal")

    out_ap = dram("out", (T, C), F32, "ExternalOutput")

    tap_aps = {}
    for tname in taps:
        sh = (128, 64) if tname == "dbg" else (T, C)
        tap_aps[tname] = dram("tap_" + tname, sh, F32, "ExternalOutput")

    with ExitStack() as root:
        def sb(name, shape, dt=F32, ctx=None):
            return (ctx or root).enter_context(nc.sbuf_tensor(name, list(shape), dt))

        def ps(name, shape, dt=F32, ctx=None):
            return (ctx or root).enter_context(nc.psum_tensor(name, list(shape), dt))

        IDENT = sb("IDENT", (128, 128))
        ONESR = sb("ONESR", (1, 128))
        ONESCB = sb("ONESCB", (128, 1), BF16)
        PC = {}
        for nm_ in ["ln1w", "ln1b", "ln2w", "ln2b", "tmk", "tmv", "tmr", "cmk",
                    "cmr", "td", "tf", "ln1wB", "ln2wB"]:
            PC[nm_] = sb("c_" + nm_, (128, 16))
        DCOL = sb("DCOL", (128, 16))
        EUCOL = sb("EUCOL", (128, 16))
        ACAR = sb("ACAR", (128, 16))
        BCAR = sb("BCAR", (128, 16))
        CXN1 = sb("CXN1", (128, 16))
        CXN2 = sb("CXN2", (128, 16))
        ONES512 = sb("ONES512", (128, TC1))
        STAT = {}
        for nm_ in ["su", "suc", "mu", "muv", "m2", "m2ne", "vare", "lv", "inv",
                    "sq", "kk"]:
            STAT[nm_] = sb("st_" + nm_, (128, 1))[:, :]
        JUNK = None
        CONSTS = sb("CONSTS", (128, 8))
        _cmap = {}

        def cconst(val):
            val = float(val)
            if val not in _cmap:
                idx = len(_cmap)
                _cmap[val] = idx
                ap = CONSTS[:, idx:idx + 1]
                b.op("gpsimd", lambda e, a=ap, v=val: e.memset(a, v),
                     reads=[], writes=[_nm(ap)])
            return CONSTS[:, _cmap[val]:_cmap[val] + 1]

        STAT["epsc"] = cconst(EPS)
        EPS2C = cconst(EPS * EPS)

        BK = [ps(f"bank{i}", (128, 512)) for i in range(8)]

        for nm_ in PC:
            k.dma("sync", PC[nm_][:, :], dr[nm_][:, :])
        k.dma("sync", IDENT[:, :], dr["ident"][:, :])
        k.dma("sync", ONESR[:, :], dr["ones_r"][:, :])
        k.dma("gpsimd", ONESCB[:, :], dr["ones_cb"][:, :])
        k.act(DCOL[:, :], PC["td"][:, :], AF.Exp)
        k.act(DCOL[:, :], DCOL[:, :], AF.Exp, scale=-1.0)
        k.act(EUCOL[:, :], PC["tf"][:, :], AF.Exp)
        k.szero(ACAR[:, :]); k.szero(BCAR[:, :])
        k.szero(CXN1[:, :]); k.szero(CXN2[:, :])
        b.op("gpsimd", lambda e, a=ONES512[:, :]: e.memset(a, 1.0),
             reads=[], writes=[_nm(ONES512[:, :])])
        def wfirst(buf_slice, wc_slice, f32_slice):
            k.dma("gpsimd", buf_slice, f32_slice)
            k.dma("sync", wc_slice, buf_slice)

        with ExitStack() as p1:
            X0TT = sb("X0TT", (128, C), ctx=p1)
            Y0TT = sb("Y0TT", (128, C), ctx=p1)
            MUS = sb("MUS", (128, 4), ctx=p1)
            INVS = sb("INVS", (128, 4), ctx=p1)
            XC = sb("XC", (128, 4 * C), ctx=p1)
            XNT = sb("XNT", (128, NCT * (TC1 + 1)), ctx=p1)
            XK = sb("XK", (128, NCT * TC1), BF16, ctx=p1)
            XV = sb("XV", (128, NCT * TC1), BF16, ctx=p1)
            XR = sb("XR", (128, NCT * TC1), BF16, ctx=p1)
            ZT = sb("ZT", (128, NCT * TC1), BF16, ctx=p1)
            EK = sb("EK", (128, TC1), ctx=p1)
            EKV = sb("EKV", (128, TC1), ctx=p1)
            NUM = sb("NUM", (128, TC1), ctx=p1)
            DEN = sb("DEN", (128, TC1), ctx=p1)
            SRB = sb("SRB", (128, TC1), ctx=p1)
            DJ = sb("DJ", (128, TC1), ctx=p1)
            ASC = sb("ASC", (128, TC1 + 1), ctx=p1)
            BSC = sb("BSC", (128, TC1 + 1), ctx=p1)
            WKVA = sb("WKVA", (128, 3 * C), BF16, ctx=p1)
            WKVB = sb("WKVB", (128, 3 * C), BF16, ctx=p1)
            WKV = [WKVA, WKVB]
            WOA = sb("WOA", (128, 8 * 512), BF16, ctx=p1)
            WOB = sb("WOB", (128, 8 * 512), BF16, ctx=p1)
            STAT["junk"] = Y0TT[:, :]

            def p1_chunk(ch):
                t0 = ch * TC1
                for tt in range(4):
                    xs = x_in[t0 + tt * 128: t0 + (tt + 1) * 128, :]
                    k.dma("sync", X0TT[:, :], xs)
                    k.ln_stats(X0TT[:, :], STAT, C)
                    k.scopy(MUS[:, tt:tt + 1], STAT["mu"])
                    k.scopy(INVS[:, tt:tt + 1], STAT["inv"])
                    k.act(STAT["lv"], STAT["vare"], AF.Ln, bias=EPS2C,
                          scale=(1.0 + EPS))
                    k.act(STAT["kk"], STAT["lv"], AF.Exp, scale=-0.5)
                    k.ts(XC[:, tt * C:(tt + 1) * C], X0TT[:, :], STAT["mu"],
                         STAT["kk"], OP.subtract, OP.mult)
                for j in range(NCT):
                    k.scopy(XNT[:, j * (TC1 + 1): j * (TC1 + 1) + 1], CXN1[:, j:j + 1])
                pb = 0
                for tt in range(4):
                    for j in range(NCT):
                        pt = BK[6 + (pb % 2)][:, 0:128]
                        pb += 1
                        k.trans(pt, XC[:, tt * C + j * 128: tt * C + (j + 1) * 128],
                                IDENT[:, :])
                        dst = XNT[:, j * (TC1 + 1) + 1 + tt * 128:
                                  j * (TC1 + 1) + 1 + (tt + 1) * 128]
                        k.act(dst, pt, AF.Identity, scale=PC["ln1w"][:, j:j + 1],
                              bias=PC["ln1b"][:, j:j + 1])
                k.scopy(CXN1[:, :],
                        XNT[:, TC1::(TC1 + 1)])
                for j in range(NCT):
                    xn_j = XNT[:, j * (TC1 + 1) + 1: j * (TC1 + 1) + 1 + TC1]
                    xx_j = XNT[:, j * (TC1 + 1): j * (TC1 + 1) + TC1]
                    k.tt(NUM[:, :], xn_j, xx_j, OP.subtract)
                    k.stt(XK[:, j * TC1:(j + 1) * TC1], NUM[:, :],
                          PC["tmk"][:, j:j + 1], xx_j, OP.mult, OP.add)
                    k.stt(XV[:, j * TC1:(j + 1) * TC1], NUM[:, :],
                          PC["tmv"][:, j:j + 1], xx_j, OP.mult, OP.add)
                    k.stt(XR[:, j * TC1:(j + 1) * TC1], NUM[:, :],
                          PC["tmr"][:, j:j + 1], xx_j, OP.mult, OP.add)
                for j in range(NCT):
                    wb = WKV[j % 2]
                    for wi, wname in enumerate(["WkT", "WvT", "WrT"]):
                        if ch == 0:
                            wfirst(wb[:, wi * C:(wi + 1) * C],
                                   dr["wc_" + wname][j * 128:(j + 1) * 128, :],
                                   dr[wname][j * 128:(j + 1) * 128, :])
                        else:
                            k.dma("sync", wb[:, wi * C:(wi + 1) * C],
                                  dr["wc_" + wname][j * 128:(j + 1) * 128, :])
                    pk, pv, pr = BK[(j % 2) * 3], BK[(j % 2) * 3 + 1], BK[(j % 2) * 3 + 2]
                    for kk in range(NCT):
                        k.mm(pk[:, :], wb[:, 0 * C + kk * 128: 0 * C + (kk + 1) * 128],
                             XK[:, kk * TC1:(kk + 1) * TC1], start=(kk == 0), stop=(kk == 15))
                    for kk in range(NCT):
                        k.mm(pv[:, :], wb[:, 1 * C + kk * 128: 1 * C + (kk + 1) * 128],
                             XV[:, kk * TC1:(kk + 1) * TC1], start=(kk == 0), stop=(kk == 15))
                    for kk in range(NCT):
                        k.mm(pr[:, :], wb[:, 2 * C + kk * 128: 2 * C + (kk + 1) * 128],
                             XR[:, kk * TC1:(kk + 1) * TC1], start=(kk == 0), stop=(kk == 15))
                    k.act(EK[:, :], pk[:, :], AF.Exp)
                    k.act(SRB[:, :], pr[:, :], AF.Tanh, scale=0.5)
                    k.ts(SRB[:, :], SRB[:, :], 0.5, 0.5, OP.mult, OP.add)
                    k.tt(EKV[:, :], EK[:, :], pv[:, :], OP.mult)
                    k.ts(DJ[:, :], ONES512[:, :], DCOL[:, j:j + 1], None, OP.mult)
                    k.scopy(ASC[:, 0:1], ACAR[:, j:j + 1])
                    k.scopy(BSC[:, 0:1], BCAR[:, j:j + 1])
                    k.scan(ASC[:, 1:TC1 + 1], DJ[:, :], EKV[:, :], ASC[:, 0:1])
                    k.scan(BSC[:, 1:TC1 + 1], DJ[:, :], EK[:, :], BSC[:, 0:1])
                    k.scopy(ACAR[:, j:j + 1], ASC[:, TC1:TC1 + 1])
                    k.scopy(BCAR[:, j:j + 1], BSC[:, TC1:TC1 + 1])
                    k.stt(NUM[:, :], EKV[:, :], EUCOL[:, j:j + 1], ASC[:, 0:TC1],
                          OP.mult, OP.add)
                    k.stt(DEN[:, :], EK[:, :], EUCOL[:, j:j + 1], BSC[:, 0:TC1],
                          OP.mult, OP.add)
                    k.recip(DEN[:, :], DEN[:, :])
                    k.tt(NUM[:, :], NUM[:, :], DEN[:, :], OP.mult)
                    k.tt(ZT[:, j * TC1:(j + 1) * TC1], NUM[:, :], SRB[:, :], OP.mult)
                for dch in range(4):
                    if ch == 0:
                        wfirst(WOA[:, :], dr["wc_WoT"][dch * 128:(dch + 1) * 128, 0:4096],
                               dr["WoT"][dch * 128:(dch + 1) * 128, 0:4096])
                        wfirst(WOB[:, :], dr["wc_WoT"][dch * 128:(dch + 1) * 128, 4096:8192],
                               dr["WoT"][dch * 128:(dch + 1) * 128, 4096:8192])
                    else:
                        k.dma("sync", WOA[:, :],
                              dr["wc_WoT"][dch * 128:(dch + 1) * 128, 0:4096])
                        k.dma("sync", WOB[:, :],
                              dr["wc_WoT"][dch * 128:(dch + 1) * 128, 4096:8192])
                    for tt in range(4):
                        pw = BK[6 + (tt % 2)]
                        for kk in range(NCT):
                            wo = WOA if kk < 8 else WOB
                            k.mm(pw[:, :], ZT[:, kk * TC1 + tt * 128: kk * TC1 + (tt + 1) * 128],
                                 wo[:, (kk % 8) * 512:((kk % 8) + 1) * 512],
                                 start=(kk == 0), stop=(kk == 15))
                        xq = (NUM if (dch * 4 + tt) % 2 == 0 else DEN)[:, :]
                        k.dma("sync", xq,
                              x_in[t0 + tt * 128: t0 + (tt + 1) * 128,
                                   dch * 512:(dch + 1) * 512])
                        k.ts(xq, xq, MUS[:, tt:tt + 1],
                             INVS[:, tt:tt + 1], OP.subtract, OP.mult)
                        if "y0r" in tap_aps:
                            k.dma("gpsimd",
                                  tap_aps["y0r"][t0 + tt * 128: t0 + (tt + 1) * 128,
                                                 dch * 512:(dch + 1) * 512], xq)
                        dst = XC[:, tt * C + dch * 512: tt * C + (dch + 1) * 512]
                        k.tt(dst, xq, pw[:, :], OP.add)
                if "x1a" in tap_aps:
                    for tt in range(4):
                        k.dma("sync", tap_aps["x1a"][t0 + tt * 128: t0 + (tt + 1) * 128, :],
                              XC[:, tt * C:(tt + 1) * C])
                for j in range(NCT):
                    k.scopy(XNT[:, j * (TC1 + 1): j * (TC1 + 1) + 1], CXN2[:, j:j + 1])
                for tt in range(4):
                    xt = XC[:, tt * C:(tt + 1) * C]
                    k.ln_stats(xt, STAT, C)
                    yt = Y0TT[:, :]
                    k.ts(yt, xt, STAT["mu"], STAT["inv"], OP.subtract, OP.mult)
                    for j in range(NCT):
                        pt = BK[6 + (j % 2)][:, 0:128]
                        k.trans(pt, Y0TT[:, j * 128:(j + 1) * 128],
                                IDENT[:, :])
                        dst = XNT[:, j * (TC1 + 1) + 1 + tt * 128:
                                  j * (TC1 + 1) + 1 + (tt + 1) * 128]
                        k.act(dst, pt, AF.Identity, scale=PC["ln2w"][:, j:j + 1],
                              bias=PC["ln2b"][:, j:j + 1])
                k.scopy(CXN2[:, :], XNT[:, TC1::(TC1 + 1)])
                for j in range(NCT):
                    xn_j = XNT[:, j * (TC1 + 1) + 1: j * (TC1 + 1) + 1 + TC1]
                    xx_j = XNT[:, j * (TC1 + 1): j * (TC1 + 1) + TC1]
                    k.tt(NUM[:, :], xn_j, xx_j, OP.subtract)
                    k.stt(XK[:, j * TC1:(j + 1) * TC1], NUM[:, :],
                          PC["cmk"][:, j:j + 1], xx_j, OP.mult, OP.add)
                    k.stt(XR[:, j * TC1:(j + 1) * TC1], NUM[:, :],
                          PC["cmr"][:, j:j + 1], xx_j, OP.mult, OP.add)
                for jh in range(2):
                    r0, r1 = jh * 1024, (jh + 1) * 1024
                    ckdst = dr["sp_ck"][r0:r1, t0:t0 + TC1].rearrange(
                        "(j p) t -> p j t", p=128)
                    crdst = dr["sp_cr"][r0:r1, t0:t0 + TC1].rearrange(
                        "(j p) t -> p j t", p=128)
                    k.dma("sync", ckdst,
                          XK[:, jh * 8 * TC1:(jh + 1) * 8 * TC1]
                          .rearrange("p (j t) -> p j t", j=8))
                    k.dma("sync", crdst,
                          XR[:, jh * 8 * TC1:(jh + 1) * 8 * TC1]
                          .rearrange("p (j t) -> p j t", j=8))
                for tt in range(4):
                    r0, r1 = t0 + tt * 128, t0 + (tt + 1) * 128
                    k.dma("sync", X0TT[:, :], x_in[r0:r1, :])
                    k.tt(X0TT[:, :], X0TT[:, :], XC[:, tt * C:(tt + 1) * C], OP.add)
                    k.dma("gpsimd", dr["sp_xsum"][r0:r1, :], X0TT[:, :])
                    if "xsum" in tap_aps:
                        k.dma("gpsimd", tap_aps["xsum"][r0:r1, :], X0TT[:, :])
                if "y0" in tap_aps:
                    pass

            for ch in range(NCH1):
                p1_chunk(ch)
            b.barrier()

        _emit_tail(k, b, nc, dr, T, out_ap, tap_aps,
                   sb, ps, BK, IDENT, ONESR, ONESCB, PC, STAT, cconst)

        b.finalize()
        sems = {}
        for key in b.sem_keys():
            sems[key] = root.enter_context(nc.semaphore("sem_" + key))
        blk = root.enter_context(nc.Block())

        @blk.sync
        def _(eng):
            b.run("sync", eng, sems)

        @blk.gpsimd
        def _(eng):
            b.run("gpsimd", eng, sems)

        @blk.tensor
        def _(eng):
            b.run("tensor", eng, sems)

        @blk.vector
        def _(eng):
            b.run("vector", eng, sems)

        @blk.scalar
        def _(eng):
            b.run("scalar", eng, sems)

    return nc


def _emit_tail(k, b, nc, dr, T, out_ap, tap_aps, sb, ps, BK,
               IDENT, ONESR, ONESCB, PC, STAT, cconst):
    NCH = T // TC1
    NTT = T // 128

    def wfirst(buf_slice, wc_slice, f32_slice):
        k.dma("gpsimd", buf_slice, f32_slice)
        k.dma("sync", wc_slice, buf_slice)

    with ExitStack() as p2:
        CKB = sb("CKB", (128, NCT * TC1), BF16, ctx=p2)
        CRB = sb("CRB", (128, NCT * TC1), BF16, ctx=p2)
        HB = sb("HB", (128, 64 * TC1), BF16, ctx=p2)
        XS2 = sb("XS2", (128, 4 * C), ctx=p2)
        SR2 = sb("SR2", (128, TC1), ctx=p2)
        RKV = sb("RKV", (128, TC1), ctx=p2)
        SR22 = [sb(f"SR22_{i}", (128, TC1), ctx=p2) for i in range(2)]
        RKV2 = [sb(f"RKV2_{i}", (128, TC1), ctx=p2) for i in range(2)]
        WCK = [sb(f"WCK{i}", (128, NCT * 128), BF16, ctx=p2) for i in range(2)]
        WCR = [sb(f"WCR{i}", (128, NCT * 128), BF16, ctx=p2) for i in range(2)]
        WCV = [sb(f"WCV{i}", (128, 64 * 128), BF16, ctx=p2) for i in range(2)]
        for ch in range(NCH):
            t0 = ch * TC1
            for jh in range(2):
                r0, r1 = jh * 1024, (jh + 1) * 1024
                k.dma("sync",
                      CKB[:, jh * 8 * TC1:(jh + 1) * 8 * TC1]
                      .rearrange("p (j t) -> p j t", j=8),
                      dr["sp_ck"][r0:r1, t0:t0 + TC1].rearrange(
                          "(j p) t -> p j t", p=128))
                k.dma("sync",
                      CRB[:, jh * 8 * TC1:(jh + 1) * 8 * TC1]
                      .rearrange("p (j t) -> p j t", j=8),
                      dr["sp_cr"][r0:r1, t0:t0 + TC1].rearrange(
                          "(j p) t -> p j t", p=128))
            k.dma("sync", XS2[:, :].rearrange("p (tt c) -> p tt c", tt=4),
                  dr["sp_xsum"][t0:t0 + TC1, :].rearrange("(tt p) c -> p tt c", p=128))
            for m in range(64):
                wb = WCK[m % 2]
                if ch == 0:
                    wfirst(wb[:, :], dr["wc_WckT"][m * 128:(m + 1) * 128, :],
                           dr["WckT"][m * 128:(m + 1) * 128, :])
                else:
                    k.dma("sync", wb[:, :], dr["wc_WckT"][m * 128:(m + 1) * 128, :])
                ph = BK[m % 2]
                for kk in range(NCT):
                    k.mm(ph[:, :], wb[:, kk * 128:(kk + 1) * 128],
                         CKB[:, kk * TC1:(kk + 1) * TC1],
                         start=(kk == 0), stop=(kk == 15))
                k.act(RKV[:, :], ph[:, :], AF.Relu)
                k.tt(HB[:, m * TC1:(m + 1) * TC1], RKV[:, :], RKV[:, :], OP.mult)
            def p2_tr(jj):
                rk = RKV2[jj % 2]
                for tt in range(4):
                    pt = BK[6 + ((jj + tt) % 2)][:, 0:128]
                    k.trans(pt, rk[:, tt * 128:(tt + 1) * 128], IDENT[:, :])
                    dst = XS2[:, tt * C + jj * 128: tt * C + (jj + 1) * 128]
                    k.tt(dst, dst, pt, OP.add)
            for j in range(NCT):
                wcr = WCR[j % 2]
                if ch == 0:
                    wfirst(wcr[:, :], dr["wc_WcrT"][j * 128:(j + 1) * 128, :],
                           dr["WcrT"][j * 128:(j + 1) * 128, :])
                else:
                    k.dma("sync", wcr[:, :], dr["wc_WcrT"][j * 128:(j + 1) * 128, :])
                pr2 = BK[2 + (j % 2)]
                for kk in range(NCT):
                    k.mm(pr2[:, :], wcr[:, kk * 128:(kk + 1) * 128],
                         CRB[:, kk * TC1:(kk + 1) * TC1],
                         start=(kk == 0), stop=(kk == 15))
                k.act(SR22[j % 2][:, :], pr2[:, :], AF.Sigmoid)
                wcv = WCV[j % 2]
                if ch == 0:
                    wfirst(wcv[:, :], dr["wc_WcvT"][j * 128:(j + 1) * 128, :],
                           dr["WcvT"][j * 128:(j + 1) * 128, :])
                else:
                    k.dma("sync", wcv[:, :], dr["wc_WcvT"][j * 128:(j + 1) * 128, :])
                pkv = BK[4 + (j % 2)]
                for hk in range(64):
                    k.mm(pkv[:, :], wcv[:, hk * 128:(hk + 1) * 128],
                         HB[:, hk * TC1:(hk + 1) * TC1],
                         start=(hk == 0), stop=(hk == 63))
                k.tt(RKV2[j % 2][:, :], SR22[j % 2][:, :], pkv[:, :], OP.mult)
                if j > 0:
                    p2_tr(j - 1)
            p2_tr(NCT - 1)
            k.dma("sync",
                  dr["sp_x2"][t0:t0 + TC1, :].rearrange("(tt p) c -> p tt c", p=128),
                  XS2[:, :].rearrange("p (tt c) -> p tt c", tt=4))
            if "x2" in tap_aps:
                k.dma("sync",
                      tap_aps["x2"][t0:t0 + TC1, :].rearrange("(tt p) c -> p tt c", p=128),
                      XS2[:, :].rearrange("p (tt c) -> p tt c", tt=4))
        b.barrier()

    with ExitStack() as p2b:
        X2C = sb("X2C", (128, 4 * C), ctx=p2b)
        XAT = sb("XAT", (128, NCT * TC1), BF16, ctx=p2b)
        KVDT = sb("KVDT", (128, 4 * TC1), BF16, ctx=p2b)
        QDT = sb("QDT", (128, 4 * TC1), BF16, ctx=p2b)
        CSC = [sb(f"CSC{i}", (128, TC1), ctx=p2b) for i in range(2)]
        SSC = [sb(f"SSC{i}", (128, TC1), ctx=p2b) for i in range(2)]
        KRL = {w: sb(f"KRL_{w}", (128, 4 * TC1), ctx=p2b) for w in ("k", "q")}
        KRH = {w: sb(f"KRH_{w}", (128, 4 * TC1), ctx=p2b) for w in ("k", "q")}
        OLO = {w: sb(f"OLO_{w}", (128, 4 * TC1), BF16, ctx=p2b) for w in ("k", "q")}
        OHI = {w: sb(f"OHI_{w}", (128, 4 * TC1), BF16, ctx=p2b) for w in ("k", "q")}
        TMA = [sb(f"TMA{i}", (128, TC1), ctx=p2b) for i in range(2)]
        TMB = [sb(f"TMB{i}", (128, TC1), ctx=p2b) for i in range(2)]
        KP = [sb(f"KP{i}", (128, TC1), BF16, ctx=p2b) for i in range(2)]
        VTT = [sb(f"VTT{i}", (128, C), BF16, ctx=p2b) for i in range(2)]
        NR2 = 3
        WR2 = [sb(f"W2R{i}", (128, 8192), BF16, ctx=p2b) for i in range(NR2)]
        _wr2 = [0]

        def wload2(src_ap, ncols=8192, src32=None):
            buf = WR2[_wr2[0] % NR2]
            _wr2[0] += 1
            for i in range(2):
                p0, p1 = i * 64, (i + 1) * 64
                q = ("sync", "gpsimd")[i]
                if src32 is not None:
                    wfirst(buf[p0:p1, 0:ncols], src_ap[p0:p1, :], src32[p0:p1, :])
                else:
                    k.dma(q, buf[p0:p1, 0:ncols], src_ap[p0:p1, :])
            return buf

        JK2 = sb("JK2", (128, C), ctx=p2b)
        ST2 = []
        for i in range(2):
            d = {}
            for nm_ in ["su", "suc", "mu", "muv", "m2", "vare", "lv", "inv",
                        "sq", "kk"]:
                d[nm_] = sb(f"st2_{i}_{nm_}", (128, 1), ctx=p2b)[:, :]
            d["epsc"] = STAT["epsc"]
            d["junk"] = JK2[:, :]
            ST2.append(d)
        eps2c = cconst(EPS * EPS)
        for ch in range(NCH):
            t0 = ch * TC1
            csc, ssc = CSC[ch % 2], SSC[ch % 2]
            k.dma("sync", X2C[:, :].rearrange("p (tt c) -> p tt c", tt=4),
                  dr["sp_x2"][t0:t0 + TC1, :].rearrange("(tt p) c -> p tt c", p=128))
            k.dma("sync", csc[:, :], dr["cosrep"][:, t0:t0 + TC1])
            k.dma("sync", ssc[:, :], dr["sinrep"][:, t0:t0 + TC1])
            for tt in range(4):
                st = ST2[tt % 2]
                xt = X2C[:, tt * C:(tt + 1) * C]
                k.ln_stats(xt, st, C)
                k.act(st["lv"], st["vare"], AF.Ln, bias=eps2c, scale=(1.0 + EPS))
                k.act(st["kk"], st["lv"], AF.Exp, scale=-0.5)
                k.ts(xt, xt, st["mu"], st["kk"], OP.subtract, OP.mult)
            for tt in range(4):
                for j in range(NCT):
                    pt = BK[6 + (j % 2)][:, 0:128]
                    k.trans(pt, X2C[:, tt * C + j * 128: tt * C + (j + 1) * 128],
                            IDENT[:, :])
                    dst = XAT[:, j * TC1 + tt * 128: j * TC1 + (tt + 1) * 128]
                    k.act(dst, pt, AF.Identity, scale=PC["ln1wB"][:, j:j + 1])
            for wname, dst in [("kvdW", KVDT), ("qdW", QDT)]:
                wb = wload2(dr["wc_" + wname][0:128, :],
                            src32=dr[wname][0:128, :] if ch == 0 else None)
                for m4 in range(4):
                    pp = BK[m4 % 2]
                    for kk in range(NCT):
                        k.mm(pp[:, :], wb[:, kk * 512 + m4 * 128: kk * 512 + (m4 + 1) * 128],
                             XAT[:, kk * TC1:(kk + 1) * TC1],
                             start=(kk == 0), stop=(kk == 15))
                    k.scopy(dst[:, m4 * TC1:(m4 + 1) * TC1], pp[:, :])
            for hh in range(2):
                wb = wload2(dr["wc_rkW"][hh * 128:(hh + 1) * 128, :],
                            src32=dr["rkW"][hh * 128:(hh + 1) * 128, :]
                            if ch == 0 else None)
                for m4 in range(4):
                    pp = BK[2 + (m4 % 2)]
                    for kk in range(NCT):
                        k.mm(pp[:, :],
                             wb[:, kk * 512 + m4 * 128: kk * 512 + (m4 + 1) * 128],
                             XAT[:, kk * TC1:(kk + 1) * TC1],
                             start=(kk == 0), stop=(kk == 15))
                    dst = (KRL["k"] if hh == 0 else KRH["k"])
                    k.scopy(dst[:, m4 * TC1:(m4 + 1) * TC1], pp[:, :])
            wb = wload2(dr["wc_rqW"][0:128, :], ncols=4096,
                        src32=dr["rqW"][0:128, :] if ch == 0 else None)
            for m8 in range(8):
                pp = BK[2 + (m8 % 2)]
                for kk in range(4):
                    k.mm(pp[:, :],
                         wb[:, kk * 1024 + m8 * 128: kk * 1024 + (m8 + 1) * 128],
                         QDT[:, kk * TC1:(kk + 1) * TC1],
                         start=(kk == 0), stop=(kk == 3))
                dst = (KRL["q"] if m8 < 4 else KRH["q"])
                k.scopy(dst[:, (m8 % 4) * TC1:((m8 % 4) + 1) * TC1], pp[:, :])
            for wname, src, spname in [("kuW", KVDT, "sp_k"), ("quW", QDT, "sp_q")]:
                wb = wload2(dr["wc_" + wname][0:128, :], ncols=4096,
                            src32=dr[wname][0:128, :] if ch == 0 else None)
                for m8 in range(8):
                    pp = BK[4 + (m8 % 2)]
                    for kk in range(4):
                        k.mm(pp[:, :], wb[:, kk * 1024 + m8 * 128: kk * 1024 + (m8 + 1) * 128],
                             src[:, kk * TC1:(kk + 1) * TC1],
                             start=(kk == 0), stop=(kk == 3))
                    kp = KP[m8 % 2]
                    k.scopy(kp[:, :], pp[:, :])
                    k.dma("sync", dr[spname][(2 * m8) * 128:(2 * m8) * 128 + 64,
                                             t0:t0 + TC1], kp[0:64, :])
                    k.dma("sync", dr[spname][(2 * m8 + 1) * 128:(2 * m8 + 1) * 128 + 64,
                                             t0:t0 + TC1], kp[64:128, :])
            wb = wload2(dr["wc_vuW"][0:128, :],
                        src32=dr["vuW"][0:128, :] if ch == 0 else None)
            for tt in range(4):
                vtt = VTT[tt % 2]
                for dch in range(4):
                    pp = BK[(dch % 2)]
                    for kk in range(4):
                        k.mm(pp[:, :], KVDT[:, kk * TC1 + tt * 128: kk * TC1 + (tt + 1) * 128],
                             wb[:, kk * C + dch * 512: kk * C + (dch + 1) * 512],
                             start=(kk == 0), stop=(kk == 3))
                    k.scopy(vtt[:, dch * 512:(dch + 1) * 512], pp[:, :])
                k.dma("sync", dr["sp_v"][t0 + tt * 128: t0 + (tt + 1) * 128, :],
                      vtt[:, :])
            for which in ("k", "q"):
                krl, krh = KRL[which], KRH[which]
                olo, ohi = OLO[which], OHI[which]
                for g in range(4):
                    lo = krl[:, g * TC1:(g + 1) * TC1]
                    hi = krh[:, g * TC1:(g + 1) * TC1]
                    ta, tb = TMA[g % 2], TMB[g % 2]
                    k.tt(ta[:, :], lo, csc[:, :], OP.mult)
                    k.tt(tb[:, :], hi, ssc[:, :], OP.mult)
                    k.tt(olo[:, g * TC1:(g + 1) * TC1], ta[:, :], tb[:, :], OP.subtract)
                    k.tt(ta[:, :], hi, csc[:, :], OP.mult)
                    k.tt(tb[:, :], lo, ssc[:, :], OP.mult)
                    k.tt(ohi[:, g * TC1:(g + 1) * TC1], ta[:, :], tb[:, :], OP.add)
                spname = "sp_k" if which == "k" else "sp_q"
                for g in range(4):
                    for h2 in range(4):
                        row = (4 * g + h2) * 128 + 64
                        k.dma("sync", dr[spname][row:row + 32, t0:t0 + TC1],
                              olo[h2 * 32:(h2 + 1) * 32, g * TC1:(g + 1) * TC1])
                        k.dma("sync", dr[spname][row + 32:row + 64, t0:t0 + TC1],
                              ohi[h2 * 32:(h2 + 1) * 32, g * TC1:(g + 1) * TC1])
        b.barrier()

    with ExitStack() as p3:
        NKT = T // 128
        NQB = T // 512
        KT = [sb(f"KT{i}", (128, T), BF16, ctx=p3) for i in range(2)]
        QT = [sb(f"QT{i}", (128, T), BF16, ctx=p3) for i in range(2)]
        VT = [sb(f"VT{i}", (128, T), BF16, ctx=p3) for i in range(2)]
        OTH = [sb(f"OTH{i}", (128, T), BF16, ctx=p3) for i in range(2)]
        KSQB = [sb(f"KSQB{i}", (128, T), BF16, ctx=p3) for i in range(2)]
        MASKB = sb("MASKB", (128, 4 * 512), BF16, ctx=p3)
        PRB = [sb(f"PRB{i}", (128, 512), BF16, ctx=p3) for i in range(4)]
        KRALL = [sb(f"KRALL{i}", (1, T), ctx=p3) for i in range(2)]
        RLBC = [sb(f"RLBC{i}", (128, 512), ctx=p3) for i in range(2)]
        DEN3 = [sb(f"DEN3{i}", (1, 512), ctx=p3) for i in range(2)]
        RED3 = [sb(f"RED3{i}", (1, 512), ctx=p3) for i in range(4)]
        NEGCBA = sb("NEGCBA", (128, NHEAD), ctx=p3)
        SB3 = []
        for i in range(2):
            d = {nm_: sb(f"sb3_{i}_{nm_}", (1, 1), ctx=p3)[:, :]
                 for nm_ in ["kmx", "kmc", "qmx", "qmc", "prd", "cbv"]}
            SB3.append(d)
        k.dma("gpsimd", MASKB[:, :], dr["masks"][:, :])
        nlc = cconst(-math.log(math.sqrt(128.0)))
        for h in range(NHEAD):
            kt, qt = KT[h % 2], QT[h % 2]
            k.dma("sync", kt[:, :], dr["sp_k"][h * 128:(h + 1) * 128, 0:T])
            k.dma("sync", qt[:, :], dr["sp_q"][h * 128:(h + 1) * 128, 0:T])
            sq = KSQB[h % 2]
            st3 = SB3[h % 2]
            kra = KRALL[h % 2]
            for which, srcb in (("k", kt), ("q", qt)):
                k.act(sq[:, :], srcb[:, :], AF.Square)
                for qb2 in range(NQB):
                    pb = BK[6 + (qb2 % 2)]
                    k.mm(pb[0:1, :], ONESCB[:, :], sq[:, qb2 * 512:(qb2 + 1) * 512])
                    k.scopy(kra[0:1, qb2 * 512:(qb2 + 1) * 512], pb[0:1, :])
                mx = st3["kmx"] if which == "k" else st3["qmx"]
                mc = st3["kmc"] if which == "k" else st3["qmc"]
                k.red(mx, kra[0:1, :], "max")
                k.scopy(mc, mx)
            k.tt(st3["prd"], st3["qmc"], st3["kmc"], OP.mult)
            k.act(st3["cbv"], st3["prd"], AF.Ln)
            k.act(st3["cbv"], st3["cbv"], AF.Exp, scale=0.5, bias=nlc[0:1, :])
            k.mm(BK[6 + (h % 2)][:, 0:1], ONESR[:, :], st3["cbv"])
            k.ts(NEGCBA[:, h:h + 1], BK[6 + (h % 2)][:, 0:1], -1.0, None, OP.mult)
        for h in range(NHEAD):
            kt, qt, vt = KT[h % 2], QT[h % 2], VT[h % 2]
            oth = OTH[h % 2]
            k.dma("sync", kt[:, :], dr["sp_k"][h * 128:(h + 1) * 128, 0:T])
            k.dma("sync", qt[:, :], dr["sp_q"][h * 128:(h + 1) * 128, 0:T])
            for kg in range(0, NKT, 4):
                k.dma("sync",
                      vt[:, kg * 128:(kg + 4) * 128]
                      .rearrange("p (kb d) -> p kb d", kb=4),
                      dr["sp_v"][kg * 128:(kg + 4) * 128, h * 128:(h + 1) * 128]
                      .rearrange("(kb p) d -> p kb d", p=128))
            SCB = [BK[0], BK[1], BK[6], BK[7]]
            for qb in range(NQB):
                rs = 2 + (qb % 2)
                bs = 4 + (qb % 2)
                kbs = list(range(4 * qb, 4 * qb + 4)) + list(range(0, 4 * qb))
                nbl = len(kbs)

                def p3_lo(i, bs=bs, rs=rs, nbl=nbl, kbs=kbs):
                    kb = kbs[i]
                    pr = PRB[i % 4]
                    k.mm(BK[rs][0:1, :], ONESCB[:, :], pr[:, :],
                         start=(i == 0), stop=(i == nbl - 1))
                    k.mm(BK[bs][:, :], vt[:, kb * 128:(kb + 1) * 128], pr[:, :],
                         start=(i == 0), stop=(i == nbl - 1))
                for i in range(nbl):
                    kb = kbs[i]
                    ps_ = SCB[i % 4]
                    k.mm(ps_[:, :], kt[:, kb * 128:(kb + 1) * 128],
                         qt[:, qb * 512:(qb + 1) * 512])
                    pr = PRB[i % 4]
                    k.act(pr[:, :], ps_[:, :], AF.Exp,
                          scale=1.0 / math.sqrt(128.0), bias=NEGCBA[:, h:h + 1])
                    off = kb - 4 * qb
                    if off >= 0:
                        k.tt(pr[:, :], pr[:, :],
                             MASKB[:, off * 512:(off + 1) * 512], OP.mult)
                    if i >= 2:
                        p3_lo(i - 2)
                if nbl >= 2:
                    p3_lo(nbl - 2)
                p3_lo(nbl - 1)
                rlbc = RLBC[qb % 2]
                dsr = DEN3[qb % 2]
                k.scopy(dsr[:, :], BK[rs][0:1, :])
                pbc = BK[rs]
                k.mm(pbc[:, :], ONESR[:, :], dsr[0:1, :])
                k.recip(rlbc[:, :], pbc[:, :])
                k.tt(oth[:, qb * 512:(qb + 1) * 512], BK[bs][:, :],
                     rlbc[:, :], OP.mult)
                k.dma("sync", dr["sp_ot"][h * 128:(h + 1) * 128,
                                          qb * 512:(qb + 1) * 512],
                      oth[:, qb * 512:(qb + 1) * 512])
        b.barrier()

    with ExitStack() as p4:
        X2C = sb("X2C4", (128, 4 * C), ctx=p4)
        HX = sb("HX", (128, 4 * C), ctx=p4)
        XMT32 = sb("XMT32", (128, NCT * TC1), ctx=p4)
        OXM = sb("OXM", (128, NCT * TC1), BF16, ctx=p4)
        HE = sb("HE", (128, 8 * TC1), BF16, ctx=p4)
        NRW = 4
        WRNG = [sb(f"WR{i}", (128, 8192), BF16, ctx=p4) for i in range(NRW)]
        _wrr = [0]

        def wload(src_ap, ncols=8192, src32=None):
            buf = WRNG[_wrr[0] % NRW]
            q = "sync" if _wrr[0] % 2 == 0 else "gpsimd"
            _wrr[0] += 1
            if src32 is not None:
                wfirst(buf[:, 0:ncols], src_ap, src32)
            else:
                k.dma(q, buf[:, 0:ncols], src_ap)
            return buf
        RWS = sb("RWS", (128, NCT * NEXP), ctx=p4)
        RB7 = sb("RB7", (1, NEXP), ctx=p4)
        B7BC = sb("B7BC", (128, NEXP), ctx=p4)
        SGR = sb("SGR", (128, NEXP), ctx=p4)
        SIGS = sb("SIGS", (128, NEXP), ctx=p4)
        EQ = sb("EQ", (128, NEXP), ctx=p4)
        MSK7 = sb("MSK7", (128, NEXP), ctx=p4)
        MSEL = sb("MSEL", (128, NEXP), ctx=p4)
        MSELC = sb("MSELC", (128, NEXP), ctx=p4)
        W7A = sb("W7A", (128, NEXP), ctx=p4)
        W7AC = sb("W7AC", (128, NEXP), ctx=p4)
        W7B = sb("W7B", (128, NEXP), ctx=p4)
        W7C = sb("W7C", (128, 4 * NEXP), ctx=p4)
        SGF = sb("SGF", (128, TC1), ctx=p4)
        SM1 = {}
        for nm_ in ["b1", "b1c", "cnt", "cntc", "b2", "b2c", "ge2", "ge2c",
                    "d12", "d12c", "adj", "adjc", "b2f", "b2fc", "den",
                    "denc", "rd", "rd8"]:
            SM1[nm_] = sb("sm_" + nm_, (128, 1), ctx=p4)[:, :]
        JUNK4 = sb("JUNK4", (128, C), ctx=p4)
        STAT["junk"] = JUNK4[:, :]
        k.dma("sync", RWS[:, :], dr["routerW"][:, :])
        k.dma("sync", RB7[:, :], dr["routerB7"][:, :])
        k.mm(BK[5][:, 0:NEXP], ONESR[:, :], RB7[:, :])
        k.scopy(B7BC[:, :], BK[5][:, 0:NEXP])
        for ch in range(NCH):
            t0 = ch * TC1
            k.dma("sync", X2C[:, :].rearrange("p (tt c) -> p tt c", tt=4),
                  dr["sp_x2"][t0:t0 + TC1, :].rearrange("(tt p) c -> p tt c", p=128))
            for jh in range(2):
                r0, r1 = jh * 1024, (jh + 1) * 1024
                k.dma("sync",
                      OXM[:, jh * 8 * TC1:(jh + 1) * 8 * TC1]
                      .rearrange("p (j t) -> p j t", j=8),
                      dr["sp_ot"][r0:r1, t0:t0 + TC1].rearrange(
                          "(j p) t -> p j t", p=128))
            for tt in range(4):
                xt = X2C[:, tt * C:(tt + 1) * C]
                k.ln_stats(xt, STAT, C)
                k.ts(HX[:, tt * C:(tt + 1) * C], xt, STAT["mu"], STAT["inv"],
                     OP.subtract, OP.mult)
            for dch in range(4):
                wow = wload(dr["wc_owW"][dch * 128:(dch + 1) * 128, :],
                            src32=dr["owW"][dch * 128:(dch + 1) * 128, :]
                            if ch == 0 else None)
                for tt in range(4):
                    pw = BK[(tt % 2)]
                    for hk in range(NCT):
                        k.mm(pw[:, :], OXM[:, hk * TC1 + tt * 128: hk * TC1 + (tt + 1) * 128],
                             wow[:, hk * 512:(hk + 1) * 512],
                             start=(hk == 0), stop=(hk == 15))
                    dst = HX[:, tt * C + dch * 512: tt * C + (dch + 1) * 512]
                    k.tt(dst, dst, pw[:, :], OP.add)
            if "h" in tap_aps:
                for tt in range(4):
                    k.dma("sync", tap_aps["h"][t0 + tt * 128: t0 + (tt + 1) * 128, :],
                          HX[:, tt * C:(tt + 1) * C])
            for tt in range(4):
                k.tt(X2C[:, tt * C:(tt + 1) * C], X2C[:, tt * C:(tt + 1) * C],
                     HX[:, tt * C:(tt + 1) * C], OP.add)
            for tt in range(4):
                ht = HX[:, tt * C:(tt + 1) * C]
                k.ln_stats(ht, STAT, C, rms=True)
                k.ts(ht, ht, STAT["inv"], None, OP.mult)
            for tt in range(4):
                for j in range(NCT):
                    pt = BK[6 + (j % 2)][:, 0:128]
                    k.trans(pt, HX[:, tt * C + j * 128: tt * C + (j + 1) * 128],
                            IDENT[:, :])
                    dst = XMT32[:, j * TC1 + tt * 128: j * TC1 + (tt + 1) * 128]
                    k.ts(dst, pt, PC["ln2wB"][:, j:j + 1], None, OP.mult)
            k.scopy(OXM[:, :], XMT32[:, :])
            for tt in range(4):
                prr = BK[5][:, 0:NEXP]
                for kk in range(NCT):
                    k.mm(prr, XMT32[:, kk * TC1 + tt * 128: kk * TC1 + (tt + 1) * 128],
                         RWS[:, kk * NEXP:(kk + 1) * NEXP],
                         start=(kk == 0), stop=(kk == 15))
                k.tt(SGR[:, :], prr, B7BC[:, :], OP.add)
                k.act(SIGS[:, :], SGR[:, :], AF.Sigmoid)
                k.red(SM1["b1"], SIGS[:, :], "max")
                k.scopy(SM1["b1c"], SM1["b1"])
                k.ts(EQ[:, :], SIGS[:, :], SM1["b1c"], None, OP.is_equal)
                k.scopy(MSK7[:, :], EQ[:, :])
                k.red(SM1["cnt"], MSK7[:, :], "sum")
                k.scopy(SM1["cntc"], SM1["cnt"])
                k.stt(MSK7[:, :], MSK7[:, :], -1e30, SIGS[:, :], OP.mult, OP.add)
                k.scopy(EQ[:, :], MSK7[:, :])
                k.red(SM1["b2"], EQ[:, :], "max")
                k.scopy(SM1["b2c"], SM1["b2"])
                k.ts(SM1["ge2"], SM1["cntc"], 1.5, None, OP.is_ge)
                k.scopy(SM1["ge2c"], SM1["ge2"])
                k.tt(SM1["d12"], SM1["b1c"], SM1["b2c"], OP.subtract)
                k.scopy(SM1["d12c"], SM1["d12"])
                k.tt(SM1["adj"], SM1["ge2c"], SM1["d12c"], OP.mult)
                k.scopy(SM1["adjc"], SM1["adj"])
                k.tt(SM1["b2f"], SM1["b2c"], SM1["adjc"], OP.add)
                k.scopy(SM1["b2fc"], SM1["b2f"])
                k.tt(SM1["den"], SM1["b1c"], SM1["b2fc"], OP.add)
                k.scopy(SM1["denc"], SM1["den"])
                k.recip(SM1["rd"], SM1["denc"])
                k.scopy(SM1["rd8"], SM1["rd"])
                k.smul(SM1["rd8"], SM1["rd8"], 0.125)
                k.ts(MSEL[:, :], SIGS[:, :], SM1["b2fc"], None, OP.is_ge)
                k.scopy(MSELC[:, :], MSEL[:, :])
                k.tt(W7A[:, :], SIGS[:, :], MSELC[:, :], OP.mult)
                k.scopy(W7AC[:, :], W7A[:, :])
                k.ts(W7B[:, :], W7AC[:, :], SM1["rd8"], None, OP.mult)
                k.scopy(W7C[:, tt * NEXP:(tt + 1) * NEXP], W7B[:, :])
            for e in range(8):
                if e < NEXP:
                    gsrc = dr["wc_rG"][e, :, :]; usrc = dr["wc_rU"][e, :, :]
                    dsrc = dr["wc_rD"][e, :, :]
                    g32 = dr["rG"][e, :, :]; u32 = dr["rU"][e, :, :]
                    d32 = dr["rD"][e, :, :]
                else:
                    gsrc = dr["wc_shG"][:, :]; usrc = dr["wc_shU"][:, :]
                    dsrc = dr["wc_shD"][:, :]
                    g32 = dr["shG"][:, :]; u32 = dr["shU"][:, :]
                    d32 = dr["shD"][:, :]
                fst = (ch == 0)
                for half in range(2):
                    wg = wload(gsrc[half * 128:(half + 1) * 128, :],
                               src32=g32[half * 128:(half + 1) * 128, :] if fst else None)
                    wu = wload(usrc[half * 128:(half + 1) * 128, :],
                               src32=u32[half * 128:(half + 1) * 128, :] if fst else None)
                    for m4 in range(4):
                        im = half * 4 + m4
                        pg = BK[2 + 4 * (im % 2)]; pu = BK[3 + 4 * (im % 2)]
                        for kk in range(NCT):
                            k.mm(pg[:, :], wg[:, kk * 512 + m4 * 128: kk * 512 + (m4 + 1) * 128],
                                 OXM[:, kk * TC1:(kk + 1) * TC1],
                                 start=(kk == 0), stop=(kk == 15))
                        k.act(SGF[:, :], pg[:, :], AF.Silu)
                        for kk in range(NCT):
                            k.mm(pu[:, :], wu[:, kk * 512 + m4 * 128: kk * 512 + (m4 + 1) * 128],
                                 OXM[:, kk * TC1:(kk + 1) * TC1],
                                 start=(kk == 0), stop=(kk == 15))
                        k.tt(HE[:, im * TC1:(im + 1) * TC1], SGF[:, :], pu[:, :],
                             OP.mult)
                for cch in range(4):
                    wd = wload(dsrc[cch * 128:(cch + 1) * 128, :], ncols=4096,
                               src32=d32[cch * 128:(cch + 1) * 128, :] if fst else None)
                    for tt in range(4):
                        pd = BK[4] if tt % 2 == 0 else BK[5]
                        for ik in range(8):
                            k.mm(pd[:, :], HE[:, ik * TC1 + tt * 128: ik * TC1 + (tt + 1) * 128],
                                 wd[:, ik * 512:(ik + 1) * 512],
                                 start=(ik == 0), stop=(ik == 7))
                        dst = X2C[:, tt * C + cch * 512: tt * C + (cch + 1) * 512]
                        if e < NEXP:
                            k.stt(dst, pd[:, :], W7C[:, tt * NEXP + e: tt * NEXP + e + 1],
                                  dst, OP.mult, OP.add)
                        else:
                            k.stt(dst, pd[:, :], 0.125, dst, OP.mult, OP.add)
            k.dma("sync",
                  out_ap[t0:t0 + TC1, :].rearrange("(tt p) c -> p tt c", p=128),
                  X2C[:, :].rearrange("p (tt c) -> p tt c", tt=4))
        b.barrier()



def kernel(**inputs):
    T = T_TOKENS
    x = np.asarray(inputs["L0_attn_x"], np.float32)
    g = _host_prep(inputs, T)
    nc = build_program(T, taps=DEBUG_TAPS)
    in_maps = []
    for bq in range(NCORES):
        m = dict(g)
        m["x"] = np.ascontiguousarray(x[bq, :T, :])
        in_maps.append(m)
    res = run_bass_kernel_spmd(nc, in_maps, core_ids=list(range(NCORES)))
    out = np.stack([res.results[bq]["out"] for bq in range(NCORES)], axis=0)
    return out
